# version 11
# baseline (speedup 1.0000x reference)
import numpy as np
import ml_dtypes
from contextlib import ExitStack

import concourse.bass as bass
import concourse.mybir as mybir
from concourse.bass_utils import run_bass_kernel_spmd

F32 = mybir.dt.float32
BF16 = mybir.dt.bfloat16
I32 = mybir.dt.int32
AF = mybir.ActivationFunctionType
ALU = mybir.AluOpType
AX = mybir.AxisListType
BF = ml_dtypes.bfloat16

D = 2048
T = 2048
NCH = D // 128
NTC = T // 128
NCORE = 8



class Sem:
    def __init__(self, h):
        self.h = h
        self.v = 0

    def on(self, instr, k=1):
        instr.then_inc(self.h, k)
        self.v += k
        return self.v


class Env:
    def __init__(self, nc, ctx):
        self.nc = nc
        self.ctx = ctx
        self.sems = {}

    PREFIXES = ("ln_", "qk_", "at_", "wo_", "w1_", "mo_", "p1_", "g1_", "d1_", "f1_")

    def sem(self, name):
        for p in self.PREFIXES:
            if name.startswith(p):
                name = name[len(p):]
                break
        if name not in self.sems:
            self.sems[name] = Sem(self.ctx.enter_context(self.nc.semaphore(name)))
        return self.sems[name]


class Rot:

    def __init__(self, env, name, nslots):
        self.sem = env.sem(name)
        self.n = nslots
        self.uses = 0
        self.waits = []

    def acquire(self, *engs):
        u = self.uses
        self.uses += 1
        if u >= self.n:
            s, v = self.waits[u - self.n]
            for eng in engs:
                eng.wait_ge(s, v)
        return u % self.n

    def done(self, instr):
        v = self.sem.on(instr, 1)
        self.waits.append((self.sem.h, v))

    def done_at(self, sem, v):
        self.waits.append((sem.h, v))


class SlotSem:

    def __init__(self, env, name, n):
        self.sems = [env.sem(f"{name}{i}") for i in range(n)]
        self.tgt = [0] * n
        self.n = n

    def inc(self, slot, instr):
        sem = self.sems[slot]
        sem.on(instr, 16)
        self.tgt[slot] = sem.v

    def wait(self, eng, slot):
        eng.wait_ge(self.sems[slot].h, self.tgt[slot])

    def pair(self, slot):
        return (self.sems[slot], self.tgt[slot])


def fm_view(dram, c0=0, ncols=None, t0=0, nt=None):
    ap = dram.ap() if hasattr(dram, "ap") else dram
    F = ap.shape[0]
    nchunk = F // 128
    v = ap.rearrange("(c p) t -> p c t", p=128)
    ncols = nchunk if ncols is None else ncols
    nt = v.shape[2] - t0 if nt is None else nt
    return v[:, c0:c0 + ncols, t0:t0 + nt]



def _cols(v):
    return np.ascontiguousarray(np.asarray(v, np.float32).reshape(-1, 128).T)


def stage(inputs):
    st = {}
    x0 = np.asarray(inputs["L0_attn_x"], np.float32)
    st["x0"] = [np.ascontiguousarray(x0[i].T) for i in range(NCORE)]

    st["l0_wqkv_t"] = np.ascontiguousarray(np.asarray(inputs["L0_attn_w_qkv"], np.float32).T)
    st["l0_bqk"] = _cols(np.asarray(inputs["L0_attn_b_qkv"], np.float32)[:2560])
    bv = np.asarray(inputs["L0_attn_b_qkv"], np.float32)[2560:]
    st["l0_bv_bc"] = np.ascontiguousarray(np.broadcast_to(bv[None, :], (128, 512)))
    st["l0_wout_t"] = np.ascontiguousarray(np.asarray(inputs["L0_attn_w_out"], np.float32).T)
    st["l0_bout"] = _cols(inputs["L0_attn_b_out"])

    st["l0_router_t"] = np.ascontiguousarray(np.asarray(inputs["L0_ffn_router_w"], np.float32).T)
    st["l0_ebias_bc"] = np.ascontiguousarray(
        np.broadcast_to(np.asarray(inputs["L0_ffn_e_bias"], np.float32)[None, :], (128, 16)))
    gup = np.asarray(inputs["L0_ffn_gate_up_proj"], np.float32)
    dwn = np.asarray(inputs["L0_ffn_down_proj"], np.float32)
    sg = np.asarray(inputs["L0_ffn_shared_gate_w"], np.float32)
    su = np.asarray(inputs["L0_ffn_shared_up_w"], np.float32)
    sd = np.asarray(inputs["L0_ffn_shared_down_w"], np.float32)
    st["l0_gup_t"] = np.ascontiguousarray(np.concatenate(
        [gup.transpose(0, 2, 1), np.concatenate([sg, su], axis=0).T[None]], axis=0))
    st["l0_dwn_t"] = np.ascontiguousarray(np.concatenate(
        [dwn.transpose(0, 2, 1), sd.T[None]], axis=0))

    for nm, key in [("l1_wq", "L1_attn_Wq"), ("l1_wk", "L1_attn_Wk"), ("l1_wv", "L1_attn_Wv"),
                    ("l1_wo", "L1_attn_Wo"), ("l1_wg", "L1_attn_Wg"), ("l1_wb", "L1_attn_Wb")]:
        st[nm] = np.ascontiguousarray(np.asarray(inputs[key], np.float32))
    for nm, key in [("l1_cwq", "L1_attn_conv_wq"), ("l1_cwk", "L1_attn_conv_wk"),
                    ("l1_cwv", "L1_attn_conv_wv")]:
        w = np.asarray(inputs[key], np.float32).reshape(16, 128, 4).transpose(1, 0, 2)
        st[nm] = np.ascontiguousarray(w.reshape(128, 64))
    st["l1_normw"] = np.ascontiguousarray(np.asarray(inputs["L1_attn_norm_w"], np.float32)[:, None])

    st["l1_gw_t"] = np.ascontiguousarray(np.asarray(inputs["L1_ffn_gate_w"], np.int32).T)
    st["l1_uw_t"] = np.ascontiguousarray(np.asarray(inputs["L1_ffn_up_w"], np.int32).T)
    st["l1_dw_t"] = np.ascontiguousarray(np.asarray(inputs["L1_ffn_down_w"], np.int32).T)
    st["l1_gs"] = _cols(inputs["L1_ffn_gate_scale"])
    st["l1_us"] = _cols(inputs["L1_ffn_up_scale"])
    st["l1_ds"] = _cols(inputs["L1_ffn_down_scale"])

    st["identbf"] = np.eye(128, dtype=BF)
    st["identf"] = np.eye(128, dtype=np.float32)
    st["mask_sl"] = np.ascontiguousarray(np.tril(np.ones((128, 128), np.float32), -1))
    st["mask_ui"] = np.ascontiguousarray(np.triu(np.ones((128, 128), np.float32), 0))
    bdm = np.zeros((128, 128), np.float32)
    for bb_ in range(4):
        bdm[bb_ * 32:(bb_ + 1) * 32, bb_ * 32:(bb_ + 1) * 32] = np.tril(np.ones((32, 32), np.float32), -1)
    blm = st["mask_sl"] - bdm
    st["mask_bd"] = np.ascontiguousarray(bdm)
    st["mask_bdT"] = np.ascontiguousarray(bdm.T)
    st["mask_blnegT"] = np.ascontiguousarray(-blm.T)
    sel = np.zeros((16, 16 * 128), np.float32)
    for e in range(16):
        sel[e, e * 128:(e + 1) * 128] = 1.0
    st["selmat"] = sel
    return st



def emit_ln(env, ctx, x_d, xn_d):
    nc = env.nc
    sb = lambda n, s, d=F32: ctx.enter_context(nc.sbuf_tensor(n, s, d))
    ps = lambda n, s, d=F32: ctx.enter_context(nc.psum_tensor(n, s, d))
    ld, stm, vs, ss, pe = (env.sem("ln_" + x) for x in ["ld", "st", "v", "s", "pe"])

    xt = sb("ln_xt", [128, 2 * T])
    xbf = sb("ln_xbf", [128, 2 * T], BF16)
    sqbf = sb("ln_sqbf", [128, 2 * T], BF16)
    onesb = sb("ln_onesb", [128, 1], BF16)
    onesr = sb("ln_onesr", [1, 128])
    rows = sb("ln_rows", [1, 5 * T])
    Br = sb("ln_Br", [128, T])
    Bmr = sb("ln_Bmr", [128, T])
    xnst = sb("ln_xnst", [128, 2 * T], BF16)
    tmp32 = sb("ln_tmp32", [128, T])
    eps_t = sb("ln_eps", [1, 1])
    mh = sb("ln_mh", [1, 1])

    nc.gpsimd.memset(onesb[:], 1.0)
    nc.gpsimd.memset(onesr[:], 1.0)
    nc.gpsimd.memset(eps_t[:], 1e-5)
    c0 = pe.on(nc.gpsimd.memset(mh[:], -0.5), 1)
    lds = SlotSem(env, "ln_lds", 2)

    M0, V0, R0, MR0, T0 = 0, T, 2 * T, 3 * T, 4 * T
    with ExitStack() as c2:
        s1p = c2.enter_context(nc.psum_tensor("ln_s1", [2, T], F32))
        s2p = c2.enter_context(nc.psum_tensor("ln_s2", [2, T], F32))
        vs0 = vs.v
        for c in range(NCH):
            b = c % 2
            if c >= 2:
                nc.sync.wait_ge(vs.h, vs0 + c - 1)
            lds.inc(b, nc.sync.dma_start(xt[:, b * T:(b + 1) * T], fm_view(x_d, c, 1)[:, 0, :]))
            if c >= 2:
                nc.vector.wait_ge(pe.h, c0 + (c - 1))
            lds.wait(nc.vector, b)
            vt = vs.on(nc.vector.tensor_copy(xbf[:, b * T:(b + 1) * T], xt[:, b * T:(b + 1) * T]), 1)
            nc.scalar.wait_ge(vs.h, vt)
            stq = ss.on(nc.scalar.activation(sqbf[:, b * T:(b + 1) * T], xbf[:, b * T:(b + 1) * T], AF.Square), 1)
            nc.tensor.wait_ge(ss.h, stq)
            for s in range(4):
                nc.tensor.matmul(s1p[0:1, s * 512:(s + 1) * 512], onesb[:],
                                 xbf[:, b * T + s * 512: b * T + (s + 1) * 512],
                                 start=(c == 0), stop=(c == NCH - 1), skip_group_check=True)
            for s in range(4):
                i = nc.tensor.matmul(s2p[0:1, s * 512:(s + 1) * 512], onesb[:],
                                     sqbf[:, b * T + s * 512: b * T + (s + 1) * 512],
                                     start=(c == 0), stop=(c == NCH - 1), skip_group_check=True)
            pe.on(i, 1)
        nc.vector.wait_ge(pe.h, pe.v)
        nc.vector.tensor_scalar(rows[:, M0:M0 + T], s1p[0:1, :], 1.0 / D, None, ALU.mult)
        nc.vector.tensor_mul(rows[:, T0:T0 + T], rows[:, M0:M0 + T], rows[:, M0:M0 + T])
        vt = vs.on(nc.vector.scalar_tensor_tensor(rows[:, V0:V0 + T], s2p[0:1, :], 1.0 / D,
                                                  rows[:, T0:T0 + T], ALU.mult, ALU.subtract), 1)
        nc.scalar.wait_ge(vs.h, vt)
        nc.scalar.activation(rows[:, T0:T0 + T], rows[:, V0:V0 + T], AF.Ln, bias=eps_t[:])
        stq = ss.on(nc.scalar.activation(rows[:, R0:R0 + T], rows[:, T0:T0 + T], AF.Exp, scale=mh[:]), 1)
        nc.vector.wait_ge(ss.h, stq)
        vt = vs.on(nc.vector.tensor_mul(rows[:, MR0:MR0 + T], rows[:, M0:M0 + T], rows[:, R0:R0 + T]), 1)
        nc.tensor.wait_ge(vs.h, vt)
    with ExitStack() as c2:
        brp = c2.enter_context(nc.psum_tensor("ln_brp", [128, T], F32))
        bmrp = c2.enter_context(nc.psum_tensor("ln_bmrp", [128, T], F32))
        for s in range(4):
            nc.tensor.matmul(brp[:, s * 512:(s + 1) * 512], onesr[:],
                             rows[0:1, R0 + s * 512:R0 + (s + 1) * 512], start=True, stop=True)
        for s in range(4):
            i = nc.tensor.matmul(bmrp[:, s * 512:(s + 1) * 512], onesr[:],
                                 rows[0:1, MR0 + s * 512:MR0 + (s + 1) * 512], start=True, stop=True)
        bc = pe.on(i, 1)
        nc.vector.wait_ge(pe.h, bc)
        nc.vector.tensor_copy(Br[:], brp[:])
        vt = vs.on(nc.vector.tensor_copy(Bmr[:], bmrp[:]), 1)
        nc.tensor.wait_ge(vs.h, vt)

    vbase = vs.v
    sts = SlotSem(env, "ln_sts", 2)
    for c in range(NCH):
        b = c % 2
        if c >= 2:
            nc.sync.wait_ge(vs.h, vbase + 2 * (c - 1))
        lds.inc(b, nc.sync.dma_start(xt[:, b * T:(b + 1) * T], fm_view(x_d, c, 1)[:, 0, :]))
        if c >= 2:
            sts.wait(nc.vector, b)
        lds.wait(nc.vector, b)
        vs.on(nc.vector.tensor_mul(tmp32[:], xt[:, b * T:(b + 1) * T], Br[:]), 1)
        vt = vs.on(nc.vector.tensor_sub(xnst[:, b * T:(b + 1) * T], tmp32[:], Bmr[:]), 1)
        nc.sync.wait_ge(vs.h, vt)
        sts.inc(b, nc.sync.dma_start(fm_view(xn_d, c, 1)[:, 0, :], xnst[:, b * T:(b + 1) * T]))
    for b in range(2):
        sts.wait(nc.sync, b)



def emit_l0qkv(env, ctx, xn_d, wqkv_d, bqk_d, bvbc_d, q_d, k_d, vaug_d):
    nc = env.nc
    sb = lambda n, s, d=F32: ctx.enter_context(nc.sbuf_tensor(n, s, d))
    ld, wld, stm, vs, ss, pe = (env.sem("qk_" + x) for x in ["ld", "wld", "st", "v", "s", "pe"])

    xn_sb = sb("qk_xn", [128, NCH * T], BF16)
    wbuf = sb("qk_w", [128, 2 * NCH * 128], BF16)
    vw = sb("qk_vw", [128, NCH * 512], BF16)
    bqk = sb("qk_bqk", [128, 20])
    bvbc = sb("qk_bvbc", [128, 512])
    qs = sb("qk_qs", [128, 2 * T], BF16)
    vstg = sb("qk_vstg", [128, 2 * 520], BF16)
    pq = ctx.enter_context(nc.psum_tensor("qk_pq", [128, 2 * 512], F32))
    pv = ctx.enter_context(nc.psum_tensor("qk_pv", [128, 2 * 512], F32))

    ld.on(nc.sync.dma_start(xn_sb[:].rearrange("p (c t) -> p c t", c=NCH), fm_view(xn_d)), 16)
    ld.on(nc.sync.dma_start(bqk[:], bqk_d.ap()), 16)
    ld.on(nc.sync.dma_start(bvbc[:], bvbc_d.ap()), 16)
    PL = ld.on(nc.gpsimd.dma_start(vw[:].rearrange("p (c f) -> p c f", c=NCH),
                                   wqkv_d.ap().rearrange("(c p) f -> p c f", p=128)[:, :, 2560:3072]), 16)

    wlds = SlotSem(env, "qk_wlds", 2)
    wrot = Rot(env, "qk_wrot", 2)
    qrot = Rot(env, "qk_qrot", 2)
    prot = Rot(env, "qk_prot", 2)
    sts = SlotSem(env, "qk_sts", 2)
    vsts = SlotSem(env, "qk_vsts", 2)
    for fc in range(20):
        ws = wrot.acquire(nc.gpsimd)
        wb0 = ws * NCH * 128
        wlds.inc(ws, nc.gpsimd.dma_start(
            wbuf[:, wb0:wb0 + NCH * 128].rearrange("p (c f) -> p c f", c=NCH),
            wqkv_d.ap().rearrange("(c p) f -> p c f", p=128)[:, :, fc * 128:(fc + 1) * 128]))
        qslot = qrot.acquire(nc.scalar)
        for s in range(4):
            pslot = prot.acquire(nc.tensor)
            if fc == 0 and s == 0:
                nc.tensor.wait_ge(ld.h, PL)
            wlds.wait(nc.tensor, ws)
            for kc in range(NCH):
                i = nc.tensor.matmul(pq[:, pslot * 512:(pslot + 1) * 512],
                                     wbuf[:, wb0 + kc * 128: wb0 + (kc + 1) * 128],
                                     xn_sb[:, kc * T + s * 512: kc * T + (s + 1) * 512],
                                     start=(kc == 0), stop=(kc == NCH - 1))
            pt = pe.on(i, 1)
            if s == 3:
                wrot.done_at(pe, pt)
            nc.scalar.wait_ge(pe.h, pt)
            i = nc.scalar.activation(qs[:, qslot * T + s * 512: qslot * T + (s + 1) * 512],
                                     pq[:, pslot * 512:(pslot + 1) * 512], AF.Identity,
                                     bias=bqk[:, fc:fc + 1])
            et = ss.on(i, 1)
            prot.done_at(ss, et)
        nc.sync.wait_ge(ss.h, et)
        dst, c0 = (q_d, fc) if fc < 16 else (k_d, fc - 16)
        sts.inc(qslot, nc.sync.dma_start(fm_view(dst, c0, 1)[:, 0, :], qs[:, qslot * T:(qslot + 1) * T]))
        qrot.done_at(*sts.pair(qslot))

    vrot = Rot(env, "qk_vrot", 2)
    for tc in range(NTC):
        vslot = vrot.acquire(nc.vector)
        pslot = prot.acquire(nc.tensor)
        if tc == 0:
            nc.tensor.wait_ge(ld.h, PL)
            nc.vector.wait_ge(ld.h, PL)
        for kc in range(NCH):
            i = nc.tensor.matmul(pv[:, pslot * 512:(pslot + 1) * 512],
                                 xn_sb[:, kc * T + tc * 128: kc * T + (tc + 1) * 128],
                                 vw[:, kc * 512:(kc + 1) * 512],
                                 start=(kc == 0), stop=(kc == NCH - 1))
        pt = pe.on(i, 1)
        nc.vector.wait_ge(pe.h, pt)
        nc.vector.memset(vstg[:, vslot * 520:(vslot + 1) * 520], 1.0)
        i = nc.vector.tensor_add(
            vstg[:, vslot * 520:(vslot + 1) * 520].rearrange("p (g d) -> p g d", g=8)[:, :, 0:64],
            pv[:, pslot * 512:(pslot + 1) * 512].rearrange("p (g d) -> p g d", g=8),
            bvbc[:].rearrange("p (g d) -> p g d", g=8))
        vt = vs.on(i, 1)
        prot.done_at(vs, vt)
        nc.sync.wait_ge(vs.h, vt)
        vsts.inc(vslot, nc.sync.dma_start(vaug_d.ap()[tc * 128:(tc + 1) * 128, :, :],
                                          vstg[:, vslot * 520:(vslot + 1) * 520].rearrange("p (g d) -> p g d", g=8)))
        vrot.done_at(*vsts.pair(vslot))
    for sl_ in range(2):
        sts.wait(nc.sync, sl_)
        vsts.wait(nc.sync, sl_)



def emit_l0att(env, ctx, q_d, k_d, vaug_d, o_d):
    nc = env.nc
    sb = lambda n, s, d=F32: ctx.enter_context(nc.sbuf_tensor(n, s, d))
    ld, qld, stm, vs, ss, pe, ex = (env.sem("at_" + x) for x in
                                    ["ld", "qld", "st", "v", "s", "pe", "ex"])

    k_sb = sb("at_k", [64, 8 * T], BF16)
    vaug = sb("at_vaug", [128, NTC * 520], BF16)
    qh = sb("at_qh", [64, 2 * T], BF16)
    oh = sb("at_oh", [64, 2 * T], BF16)
    pb = sb("at_pb", [128, 3 * 1024], BF16)
    rr = sb("at_rr", [1, 512])
    rhi = sb("at_rhi", [1, 512], BF16)
    rlo = sb("at_rlo", [1, 512], BF16)
    rb = sb("at_rb", [64, 512])
    onesbr = sb("at_ones", [1, 64], BF16)
    c_m20 = sb("at_m20", [128, 1])
    c_0125 = sb("at_0125", [128, 1])
    pS = ctx.enter_context(nc.psum_tensor("at_pS", [128, 4 * 512], F32))
    pPV = ctx.enter_context(nc.psum_tensor("at_pPV", [65, 2 * 512], F32))
    pB = ctx.enter_context(nc.psum_tensor("at_pB", [64, 512], F32))

    nc.gpsimd.memset(onesbr[:], 1.0)
    nc.gpsimd.memset(c_m20[:], -20.0)
    nc.gpsimd.memset(c_0125[:], 0.125)
    ld.on(nc.sync.dma_start(k_sb[:].rearrange("p (g t) -> p g t", g=8),
                            k_d.ap().rearrange("(g p) t -> p g t", p=64)), 16)
    PL = ld.on(nc.sync.dma_start(vaug[:].rearrange("p (c gd) -> p c gd", c=NTC),
                                 vaug_d.ap().rearrange("(c p) g d -> p c (g d)", p=128)), 16)

    qlds = SlotSem(env, "at_qlds", 2)
    osts = SlotSem(env, "at_osts", 2)
    qrot = Rot(env, "at_qrot", 2)
    orot = Rot(env, "at_orot", 2)
    brot = Rot(env, "at_brot", 3)
    srot = Rot(env, "at_srot", 2)
    pvrot = Rot(env, "at_pvrot", 1)
    for g in range(8):
        for r in range(4):
            h = g * 4 + r
            qslot = qrot.acquire(nc.sync)
            qlds.inc(qslot, nc.sync.dma_start(
                qh[:, qslot * T:(qslot + 1) * T],
                q_d.ap().rearrange("(c o p) t -> p (c o) t", p=64, o=2)[:, h, :]))
            oslot = orot.acquire(nc.vector)
            LAG = 1
            for sp in range(2):
                pvslot = pvrot.acquire(nc.tensor)
                ets = {}
                bslots = {}
                for step in range(NTC + LAG):
                    if step < NTC:
                        kc = step
                        sslot = srot.acquire(nc.tensor)
                        sbase = sslot * 1024
                        if h == 0 and sp == 0 and kc == 0:
                            nc.tensor.wait_ge(ld.h, PL)
                        if sp == 0 and kc == 0:
                            qlds.wait(nc.tensor, qslot)
                        nc.tensor.matmul(pS[:, sbase:sbase + 512],
                                         k_sb[:, g * T + kc * 128: g * T + (kc + 1) * 128],
                                         qh[:, qslot * T + sp * 1024: qslot * T + sp * 1024 + 512],
                                         start=True, stop=True)
                        i = nc.tensor.matmul(pS[:, sbase + 512:sbase + 1024],
                                             k_sb[:, g * T + kc * 128: g * T + (kc + 1) * 128],
                                             qh[:, qslot * T + sp * 1024 + 512: qslot * T + sp * 1024 + 1024],
                                             start=True, stop=True)
                        pt = pe.on(i, 1)
                        bslot = brot.acquire(nc.scalar)
                        bslots[kc] = bslot
                        nc.scalar.wait_ge(pe.h, pt)
                        i = nc.scalar.activation(pb[:, bslot * 1024:(bslot + 1) * 1024],
                                                 pS[:, sbase:sbase + 1024],
                                                 AF.Exp, bias=c_m20[:], scale=c_0125[:])
                        ets[kc] = ex.on(i, 1)
                        srot.done_at(ex, ets[kc])
                    if step >= LAG:
                        kc2 = step - LAG
                        nc.tensor.wait_ge(ex.h, ets[kc2])
                        nc.tensor.matmul(pPV[:, 0:512],
                                         vaug[:, kc2 * 520 + g * 65: kc2 * 520 + g * 65 + 65],
                                         pb[:, bslots[kc2] * 1024:bslots[kc2] * 1024 + 512],
                                         start=(kc2 == 0), stop=(kc2 == NTC - 1))
                        i = nc.tensor.matmul(pPV[:, 512:1024],
                                             vaug[:, kc2 * 520 + g * 65: kc2 * 520 + g * 65 + 65],
                                             pb[:, bslots[kc2] * 1024 + 512:(bslots[kc2] + 1) * 1024],
                                             start=(kc2 == 0), stop=(kc2 == NTC - 1))
                        brot.done(i)
                pvt = brot.sem.v
                nc.vector.wait_ge(brot.sem.h, pvt)
                for st2 in range(2):
                    pvsl = st2 * 512
                    nc.vector.reciprocal(rr[:], pPV[64:65, pvsl:pvsl + 512])
                    nc.vector.tensor_copy(rhi[:], rr[:])
                    vt = vs.on(nc.vector.tensor_sub(rlo[:], rr[:], rhi[:]), 1)
                    nc.tensor.wait_ge(vs.h, vt)
                    nc.tensor.matmul(pB[:], onesbr[:], rhi[:], start=True, stop=False)
                    i = nc.tensor.matmul(pB[:], onesbr[:], rlo[:], start=False, stop=True)
                    bt = pe.on(i, 1)
                    nc.vector.wait_ge(pe.h, bt)
                    nc.vector.tensor_copy(rb[:], pB[:])
                    i = nc.vector.tensor_mul(
                        oh[:, oslot * T + sp * 1024 + pvsl: oslot * T + sp * 1024 + pvsl + 512],
                        pPV[0:64, pvsl:pvsl + 512], rb[:])
                    vt = vs.on(i, 1)
                    if st2 == 1:
                        pvrot.done_at(vs, vt)
            nc.sync.wait_ge(vs.h, vt)
            osts.inc(oslot, nc.sync.dma_start(
                o_d.ap().rearrange("(c o p) t -> p (c o) t", p=64, o=2)[:, h, :],
                oh[:, oslot * T:(oslot + 1) * T]))
            orot.done_at(*osts.pair(oslot))
            qrot.done_at(*osts.pair(oslot))
    for sl_ in range(2):
        osts.wait(nc.sync, sl_)



def emit_proj_addres(env, ctx, tag, in_d, res_d, w_d, bias_d, out_d):
    nc = env.nc
    sb = lambda n, s, d=F32: ctx.enter_context(nc.sbuf_tensor(tag + n, s, d))
    ld, wld, stm, vs, pe = (env.sem(tag + x) for x in ["ld", "wld", "st", "v", "pe"])

    in_sb = sb("_in", [128, NCH * T], BF16)
    wbuf = sb("_w", [128, 2 * NCH * 128], BF16)
    bias = sb("_b", [128, 16])
    xres = sb("_xres", [128, 2 * 512])
    ostg = sb("_o", [128, 2 * T])
    po = ctx.enter_context(nc.psum_tensor(tag + "_po", [128, 2 * 512], F32))

    PL = ld.on(nc.sync.dma_start(in_sb[:].rearrange("p (c t) -> p c t", c=NCH), fm_view(in_d)), 16)
    if bias_d is not None:
        PL = ld.on(nc.sync.dma_start(bias[:], bias_d.ap()), 16)
    else:
        nc.gpsimd.memset(bias[:], 0.0)

    wlds = SlotSem(env, tag + "_wlds", 2)
    xlds = SlotSem(env, tag + "_xlds", 2)
    osts = SlotSem(env, tag + "_osts", 2)
    wrot = Rot(env, tag + "_wrot", 2)
    orot = Rot(env, tag + "_orot", 2)
    prot = Rot(env, tag + "_prot", 2)
    xrot = Rot(env, tag + "_xrot", 2)
    for fc in range(NCH):
        ws = wrot.acquire(nc.gpsimd)
        wb0 = ws * NCH * 128
        wlds.inc(ws, nc.gpsimd.dma_start(
            wbuf[:, wb0:wb0 + NCH * 128].rearrange("p (c f) -> p c f", c=NCH),
            w_d.ap().rearrange("(c p) f -> p c f", p=128)[:, :, fc * 128:(fc + 1) * 128]))
        oslot = orot.acquire(nc.vector)
        for s in range(4):
            xs = xrot.acquire(nc.sync)
            xlds.inc(xs, nc.sync.dma_start(xres[:, xs * 512:(xs + 1) * 512],
                                           fm_view(res_d, fc, 1, s * 512, 512)[:, 0, :]))
            pslot = prot.acquire(nc.tensor)
            if fc == 0 and s == 0:
                nc.tensor.wait_ge(ld.h, PL)
            wlds.wait(nc.tensor, ws)
            for kc in range(NCH):
                i = nc.tensor.matmul(po[:, pslot * 512:(pslot + 1) * 512],
                                     wbuf[:, wb0 + kc * 128: wb0 + (kc + 1) * 128],
                                     in_sb[:, kc * T + s * 512: kc * T + (s + 1) * 512],
                                     start=(kc == 0), stop=(kc == NCH - 1))
            pt = pe.on(i, 1)
            if s == 3:
                wrot.done_at(pe, pt)
            nc.vector.wait_ge(pe.h, pt)
            xlds.wait(nc.vector, xs)
            if fc == 0 and s == 0:
                nc.vector.wait_ge(ld.h, PL)
            i = nc.vector.scalar_tensor_tensor(ostg[:, oslot * T + s * 512: oslot * T + (s + 1) * 512],
                                               po[:, pslot * 512:(pslot + 1) * 512],
                                               bias[:, fc:fc + 1], xres[:, xs * 512:(xs + 1) * 512],
                                               ALU.add, ALU.add)
            vt = vs.on(i, 1)
            prot.done_at(vs, vt)
            xrot.done_at(vs, vt)
        nc.sync.wait_ge(vs.h, vt)
        osts.inc(oslot, nc.sync.dma_start(fm_view(out_d, fc, 1)[:, 0, :], ostg[:, oslot * T:(oslot + 1) * T]))
        orot.done_at(*osts.pair(oslot))
    for sl_ in range(2):
        osts.wait(nc.sync, sl_)


def emit_l0wo(env, ctx, o_d, x0_d, wout_d, bout_d, x1a_d):
    emit_proj_addres(env, ctx, "wo", o_d, x0_d, wout_d, bout_d, x1a_d)


def emit_l1wo(env, ctx, od_d, x1_d, wo_d, x2a_d):
    emit_proj_addres(env, ctx, "w1", od_d, x1_d, wo_d, None, x2a_d)



def emit_l0moe(env, ctx, xn_d, x1a_d, router_d, ebias_d, selmat_d, identf_d, gup_d, dwn_d, x1_d):
    nc = env.nc
    sb = lambda n, s, d=F32: ctx.enter_context(nc.sbuf_tensor(n, s, d))
    ld, wld, dld, stm, vs, ss, pe, gd = (env.sem("mo_" + x) for x in
                                         ["ld", "wld", "dld", "st", "v", "s", "pe", "gd"])
    HT = 1024
    xn_sb = sb("mo_xn", [128, NCH * HT], BF16)
    acc = sb("mo_acc", [128, NCH * HT])
    rout = sb("mo_rt", [128, NCH * 16], BF16)
    ebias = sb("mo_eb", [128, 16])
    selm = sb("mo_sel", [16, 16 * 128])
    identf = sb("mo_idf", [128, 128])
    comb = sb("mo_comb", [128, 8 * 16])
    zwork = sb("mo_z", [128, 16])
    mcol = sb("mo_m", [128, 4])
    maskw = sb("mo_mask", [128, 16])
    ssum = sb("mo_ssum", [128, 2])
    cfm = sb("mo_cfm", [16, HT])
    ce = sb("mo_ce", [128, HT])
    hstash = sb("mo_h", [128, 2 * 6 * HT], BF16)
    gw = sb("mo_gw", [128, 2 * NCH * 256], BF16)
    dwb = sb("mo_dw", [128, 2 * 6 * 2048], BF16)
    gsil = sb("mo_gsil", [128, 2 * 512])
    pg = ctx.enter_context(nc.psum_tensor("mo_pg", [128, 2 * 512], F32))
    pu = ctx.enter_context(nc.psum_tensor("mo_pu", [128, 2 * 512], F32))
    pd = ctx.enter_context(nc.psum_tensor("mo_pd", [128, 2 * 512], F32))
    aux = ctx.enter_context(nc.psum_tensor("mo_aux", [128, 512], F32))
    pr = aux[:, 0:16]
    pt_ = aux[0:16, 128:256]
    pce = aux[:, 0:512]

    ld.on(nc.gpsimd.dma_start(rout[:].rearrange("p (c e) -> p c e", c=NCH),
                              router_d.ap().rearrange("(c p) e -> p c e", p=128)), 16)
    ld.on(nc.sync.dma_start(ebias[:], ebias_d.ap()), 16)
    ld.on(nc.sync.dma_start(selm[:], selmat_d.ap()), 16)
    PL = ld.on(nc.sync.dma_start(identf[:], identf_d.ap()), 16)

    glds = SlotSem(env, "mo_glds", 2)
    dlds = SlotSem(env, "mo_dlds", 2)
    hlds = SlotSem(env, "mo_hlds", 2)
    grot = Rot(env, "mo_grot", 2)
    pgrot = Rot(env, "mo_pgrot", 2)
    pdrot = Rot(env, "mo_pdrot", 2)
    groups = [(0, 1), (2, 3), (4, 5), (6, 7), (8, 9), (10, 11), (12, 13), (14, 15), (16,)]
    ngrp_done = 0

    for half in range(2):
        t0 = half * HT
        if half == 1:
            nc.sync.wait_ge(stm.h, stm.v)
            nc.sync.wait_ge(pe.h, pe.v)
        hlds.inc(0, nc.sync.dma_start(xn_sb[:].rearrange("p (c t) -> p c t", c=NCH),
                                      fm_view(xn_d, 0, NCH, t0, HT)))
        hlds.inc(1, nc.sync.dma_start(acc[:].rearrange("p (c t) -> p c t", c=NCH),
                                      fm_view(x1a_d, 0, NCH, t0, HT)))
        for tc in range(8):
            if half == 0 and tc == 0:
                nc.tensor.wait_ge(ld.h, PL)
            hlds.wait(nc.tensor, 0)
            if tc > 0 or half == 1:
                nc.tensor.wait_ge(vs.h, vs.v)
            for kc in range(NCH):
                i = nc.tensor.matmul(pr, xn_sb[:, kc * HT + tc * 128: kc * HT + (tc + 1) * 128],
                                     rout[:, kc * 16:(kc + 1) * 16],
                                     start=(kc == 0), stop=(kc == NCH - 1))
            ptv = pe.on(i, 1)
            nc.scalar.wait_ge(pe.h, ptv)
            stq = ss.on(nc.scalar.activation(comb[:, tc * 16:(tc + 1) * 16], pr, AF.Sigmoid), 1)
            nc.vector.wait_ge(ss.h, stq)
            def _w(instr):
                t = vs.on(instr, 1)
                nc.vector.wait_ge(vs.h, t)
            _w(nc.vector.tensor_add(zwork[:], comb[:, tc * 16:(tc + 1) * 16], ebias[:]))
            for j in range(4):
                _w(nc.vector.reduce_max(mcol[:, j:j + 1], zwork[:], AX.X))
                if j < 3:
                    _w(nc.vector.tensor_scalar(maskw[:], zwork[:], mcol[:, j:j + 1], None, ALU.is_equal))
                    _w(nc.vector.scalar_tensor_tensor(zwork[:], maskw[:], -1e30, zwork[:], ALU.mult, ALU.add))
            _w(nc.vector.tensor_add(zwork[:], comb[:, tc * 16:(tc + 1) * 16], ebias[:]))
            _w(nc.vector.tensor_scalar(maskw[:], zwork[:], mcol[:, 3:4], None, ALU.is_ge))
            _w(nc.vector.tensor_mul(comb[:, tc * 16:(tc + 1) * 16], comb[:, tc * 16:(tc + 1) * 16], maskw[:]))
            _w(nc.vector.reduce_sum(ssum[:, 0:1], comb[:, tc * 16:(tc + 1) * 16], AX.X))
            _w(nc.vector.tensor_scalar_add(ssum[:, 0:1], ssum[:, 0:1], 1e-20))
            _w(nc.vector.reciprocal(ssum[:, 1:2], ssum[:, 0:1]))
            vt = vs.on(nc.vector.tensor_scalar(comb[:, tc * 16:(tc + 1) * 16],
                                               comb[:, tc * 16:(tc + 1) * 16],
                                               ssum[:, 1:2], None, ALU.mult), 1)
        for tc in range(8):
            if tc == 0:
                nc.tensor.wait_ge(vs.h, vs.v)
            else:
                nc.tensor.wait_ge(vs.h, vt)
            i = nc.tensor.transpose(pt_, comb[:, tc * 16:(tc + 1) * 16], identf[:])
            ptv = pe.on(i, 1)
            nc.vector.wait_ge(pe.h, ptv)
            vt = vs.on(nc.vector.tensor_copy(cfm[:, tc * 128:(tc + 1) * 128], pt_), 1)

        def emit_ce(e):
            for s2 in range(2):
                if e == 0:
                    nc.tensor.wait_ge(vs.h, vs.v)
                i = nc.tensor.matmul(pce, selm[:, e * 128:(e + 1) * 128],
                                     cfm[:, s2 * 512:(s2 + 1) * 512], start=True, stop=True)
                ptv = pe.on(i, 1)
                nc.vector.wait_ge(pe.h, ptv)
                vt = vs.on(nc.vector.tensor_copy(ce[:, s2 * 512:(s2 + 1) * 512], pce), 1)
                nc.tensor.wait_ge(vs.h, vt)

        def emit_gu(e, eg):
            par = e % 2
            for ic in range(6):
                gs_ = grot.acquire(nc.gpsimd)
                base = gs_ * NCH * 256
                glds.inc(gs_, nc.gpsimd.dma_start(
                    gw[:, base:base + NCH * 256].rearrange("p (c f) -> p c f", c=NCH)[:, :, 0:128],
                    gup_d.ap()[e].rearrange("(c p) f -> p c f", p=128)[:, :, ic * 128:(ic + 1) * 128]))
                glds.inc(gs_, nc.gpsimd.dma_start(
                    gw[:, base:base + NCH * 256].rearrange("p (c f) -> p c f", c=NCH)[:, :, 128:256],
                    gup_d.ap()[e].rearrange("(c p) f -> p c f", p=128)[:, :, 768 + ic * 128: 768 + (ic + 1) * 128]))
                for s2 in range(2):
                    pgs = pgrot.acquire(nc.tensor)
                    glds.wait(nc.tensor, gs_)
                    for kc in range(NCH):
                        nc.tensor.matmul(pg[:, pgs * 512:(pgs + 1) * 512],
                                         gw[:, base + kc * 256: base + kc * 256 + 128],
                                         xn_sb[:, kc * HT + s2 * 512: kc * HT + (s2 + 1) * 512],
                                         start=(kc == 0), stop=(kc == NCH - 1))
                    for kc in range(NCH):
                        i = nc.tensor.matmul(pu[:, pgs * 512:(pgs + 1) * 512],
                                             gw[:, base + kc * 256 + 128: base + kc * 256 + 256],
                                             xn_sb[:, kc * HT + s2 * 512: kc * HT + (s2 + 1) * 512],
                                             start=(kc == 0), stop=(kc == NCH - 1))
                    ptv = pe.on(i, 1)
                    if s2 == 1:
                        grot.done_at(pe, ptv)
                    nc.scalar.wait_ge(pe.h, ptv)
                    stq = ss.on(nc.scalar.activation(gsil[:, pgs * 512:(pgs + 1) * 512],
                                                     pg[:, pgs * 512:(pgs + 1) * 512], AF.Silu), 1)
                    nc.vector.wait_ge(ss.h, stq)
                    if ic == 0 and s2 == 0 and eg > 0:
                        nc.vector.wait_ge(gd.h, eg - 1)
                    if e < 16:
                        nc.vector.tensor_mul(gsil[:, pgs * 512:(pgs + 1) * 512],
                                             gsil[:, pgs * 512:(pgs + 1) * 512],
                                             ce[:, s2 * 512:(s2 + 1) * 512])
                    i = nc.vector.tensor_mul(
                        hstash[:, par * 6 * HT + ic * HT + s2 * 512: par * 6 * HT + ic * HT + (s2 + 1) * 512],
                        gsil[:, pgs * 512:(pgs + 1) * 512], pu[:, pgs * 512:(pgs + 1) * 512])
                    hv = vs.on(i, 1)
                    pgrot.done_at(vs, hv)
            return hv

        def emit_dwb(e, eg):
            par = e % 2
            prev = env._dwb_last.get(par)
            if prev is not None:
                nc.gpsimd.wait_ge(gd.h, prev + 1)
            env._dwb_last[par] = eg
            dlds.inc(par, nc.gpsimd.dma_start(
                dwb[:, par * 6 * 2048:(par + 1) * 6 * 2048].rearrange("p (c f) -> p c f", c=6),
                dwn_d.ap()[e].rearrange("(c p) f -> p c f", p=128)))

        def emit_down(e, hv, first_acc):
            par = e % 2
            nc.tensor.wait_ge(vs.h, hv)
            dlds.wait(nc.tensor, par)
            for s2 in range(2):
                for fc in range(NCH):
                    pds = pdrot.acquire(nc.tensor)
                    for ic in range(6):
                        i = nc.tensor.matmul(
                            pd[:, pds * 512:(pds + 1) * 512],
                            dwb[:, par * 6 * 2048 + ic * 2048 + fc * 128: par * 6 * 2048 + ic * 2048 + (fc + 1) * 128],
                            hstash[:, par * 6 * HT + ic * HT + s2 * 512: par * 6 * HT + ic * HT + (s2 + 1) * 512],
                            start=(ic == 0), stop=(ic == 5))
                    ptv = pe.on(i, 1)
                    if s2 == 1 and fc == NCH - 1:
                        gd.on(nc.tensor.nop(), 1)
                    nc.vector.wait_ge(pe.h, ptv)
                    if first_acc and s2 == 0 and fc == 0:
                        hlds.wait(nc.vector, 1)
                    i = nc.vector.tensor_add(acc[:, fc * HT + s2 * 512: fc * HT + (s2 + 1) * 512],
                                             acc[:, fc * HT + s2 * 512: fc * HT + (s2 + 1) * 512],
                                             pd[:, pds * 512:(pds + 1) * 512])
                    av = vs.on(i, 1)
                    pdrot.done_at(vs, av)

        if half == 0:
            env._dwb_last = {}
        hv_prev = None
        for e in range(17):
            eg = half * 17 + e
            if e < 16:
                emit_ce(e)
            hv = emit_gu(e, eg)
            emit_dwb(e, eg)
            if hv_prev is not None:
                emit_down(e - 1, hv_prev, first_acc=(e == 1))
            hv_prev = hv
        emit_down(16, hv_prev, first_acc=False)
        nc.sync.wait_ge(vs.h, vs.v)
        stm.on(nc.sync.dma_start(fm_view(x1_d, 0, NCH, t0, HT),
                                 acc[:].rearrange("p (c t) -> p c t", c=NCH)), 16)
    nc.sync.wait_ge(stm.h, stm.v)



def emit_l1proj(env, ctx, xn_d, wq_d, wk_d, wv_d, wb_d, cwq_d, cwk_d, cwv_d,
                qn_d, kn_d, v_d, beta_d):
    nc = env.nc
    sb = lambda n, s, d=F32: ctx.enter_context(nc.sbuf_tensor(n, s, d))
    ld, wld, stm, vs, ss, pe = (env.sem("p1_" + x) for x in ["ld", "wld", "st", "v", "s", "pe"])

    xn_sb = sb("p1_xn", [128, NCH * T], BF16)
    wbuf = sb("p1_w", [128, 2 * NCH * 128], BF16)
    wbsb = sb("p1_wb", [128, NCH * 16], BF16)
    cw = sb("p1_cw", [128, 3 * 64])
    raw = sb("p1_raw", [128, 2 * (T + 3)])
    conv = sb("p1_conv", [128, T])
    xbf = sb("p1_xbf", [128, T], BF16)
    sqb = sb("p1_sq", [128, T], BF16)
    lnrow = sb("p1_lnr", [1, 512])
    rstd_bf = sb("p1_rbf", [1, 512], BF16)
    onesb = sb("p1_onesb", [128, 1], BF16)
    onesrb = sb("p1_onesrb", [1, 128], BF16)
    outst = sb("p1_out", [128, 2 * T], BF16)
    btst = sb("p1_bt", [128, 2 * 16])
    eps_t = sb("p1_eps", [1, 1])
    mh = sb("p1_mh", [1, 1])
    lnb_q = sb("p1_lnbq", [1, 1])
    pp = ctx.enter_context(nc.psum_tensor("p1_pp", [128, 2 * 512], F32))
    pn = ctx.enter_context(nc.psum_tensor("p1_pn", [2, 512], F32))
    pbc = ctx.enter_context(nc.psum_tensor("p1_pbc", [128, 512], F32))
    pbeta0 = ctx.enter_context(nc.psum_tensor("p1_pbeta0", [128, 16], F32))
    pbeta1 = ctx.enter_context(nc.psum_tensor("p1_pbeta1", [128, 16], F32))
    pbetas = [pbeta0, pbeta1]

    nc.gpsimd.memset(onesb[:], 1.0)
    nc.gpsimd.memset(onesrb[:], 1.0)
    nc.gpsimd.memset(eps_t[:], 1e-6)
    nc.gpsimd.memset(mh[:], -0.5)
    nc.gpsimd.memset(lnb_q[:], float(-0.5 * np.log(128.0)))
    ld.on(nc.sync.dma_start(xn_sb[:].rearrange("p (c t) -> p c t", c=NCH), fm_view(xn_d)), 16)
    ld.on(nc.sync.dma_start(cw[:, 0:64], cwq_d.ap()), 16)
    ld.on(nc.sync.dma_start(cw[:, 64:128], cwk_d.ap()), 16)
    ld.on(nc.sync.dma_start(cw[:, 128:192], cwv_d.ap()), 16)
    PL = ld.on(nc.gpsimd.dma_start(wbsb[:].rearrange("p (c e) -> p c e", c=NCH),
                                   wb_d.ap().rearrange("(c p) e -> p c e", p=128)), 16)
    wlds = SlotSem(env, "p1_wlds", 2)
    bsts = SlotSem(env, "p1_bsts", 2)
    osts = SlotSem(env, "p1_osts", 2)

    import os as _os
    brot = Rot(env, "p1_brot", 2)
    for tc in (range(NTC) if _os.environ.get("L1PROJ_BETA", "1") == "1" else []):
        bs = brot.acquire(nc.tensor, nc.scalar)
        if tc == 0:
            nc.tensor.wait_ge(ld.h, PL)
        for kc in range(NCH):
            i = nc.tensor.matmul(pbetas[bs][:],
                                 xn_sb[:, kc * T + tc * 128: kc * T + (tc + 1) * 128],
                                 wbsb[:, kc * 16:(kc + 1) * 16],
                                 start=(kc == 0), stop=(kc == NCH - 1))
        ptv = pe.on(i, 1)
        nc.scalar.wait_ge(pe.h, ptv)
        i = nc.scalar.activation(btst[:, bs * 16:(bs + 1) * 16], pbetas[bs][:], AF.Sigmoid)
        stq = ss.on(i, 1)
        nc.sync.wait_ge(ss.h, stq)
        bsts.inc(bs, nc.sync.dma_start(beta_d.ap()[tc * 128:(tc + 1) * 128, :], btst[:, bs * 16:(bs + 1) * 16]))
        brot.done_at(*bsts.pair(bs))

    wrot = Rot(env, "p1_wrot", 2)
    rrot = Rot(env, "p1_rrot", 2)
    prot = Rot(env, "p1_prot", 2)
    orot = Rot(env, "p1_orot", 2)
    import os
    _tis = [int(x) for x in os.environ.get("L1PROJ_TI", "012")]
    for ti, (w_d, cwoff, out_d, ln) in enumerate([(wq_d, 0, qn_d, True),
                                                  (wk_d, 64, kn_d, True),
                                                  (wv_d, 128, v_d, False)]):
        if ti not in _tis:
            continue
        for fc in range(NCH):
            ws = wrot.acquire(nc.gpsimd)
            wb0 = ws * NCH * 128
            wlds.inc(ws, nc.gpsimd.dma_start(
                wbuf[:, wb0:wb0 + NCH * 128].rearrange("p (c f) -> p c f", c=NCH),
                w_d.ap().rearrange("(c p) f -> p c f", p=128)[:, :, fc * 128:(fc + 1) * 128]))
            rs = rrot.acquire(nc.vector)
            rbase = rs * (T + 3)
            for s in range(4):
                pslot = prot.acquire(nc.tensor)
                wlds.wait(nc.tensor, ws)
                for kc in range(NCH):
                    i = nc.tensor.matmul(pp[:, pslot * 512:(pslot + 1) * 512],
                                         wbuf[:, wb0 + kc * 128: wb0 + (kc + 1) * 128],
                                         xn_sb[:, kc * T + s * 512: kc * T + (s + 1) * 512],
                                         start=(kc == 0), stop=(kc == NCH - 1))
                ptv = pe.on(i, 1)
                if s == 3:
                    wrot.done_at(pe, ptv)
                nc.vector.wait_ge(pe.h, ptv)
                if s == 0:
                    nc.vector.memset(raw[:, rbase:rbase + 3], 0.0)
                    if ti == 0 and fc == 0:
                        nc.vector.wait_ge(ld.h, PL)
                i = nc.vector.tensor_copy(raw[:, rbase + 3 + s * 512: rbase + 3 + (s + 1) * 512],
                                          pp[:, pslot * 512:(pslot + 1) * 512])
                cv = vs.on(i, 1)
                prot.done_at(vs, cv)
            nc.vector.tensor_scalar(conv[:], raw[:, rbase + 0: rbase + T],
                                    cw[:, cwoff + fc * 4 + 0: cwoff + fc * 4 + 1], None, ALU.mult)
            for tap in range(1, 4):
                i = nc.vector.scalar_tensor_tensor(conv[:], raw[:, rbase + tap: rbase + tap + T],
                                                   cw[:, cwoff + fc * 4 + tap: cwoff + fc * 4 + tap + 1],
                                                   conv[:], ALU.mult, ALU.add)
            vt = vs.on(i, 1)
            rrot.done_at(vs, vt)
            oslot = orot.acquire(nc.scalar)
            nc.scalar.wait_ge(vs.h, vt)
            if not ln:
                stq = ss.on(nc.scalar.activation(outst[:, oslot * T:(oslot + 1) * T], conv[:], AF.Silu), 1)
                nc.sync.wait_ge(ss.h, stq)
            else:
                ss.on(nc.scalar.activation(conv[:], conv[:], AF.Silu), 1)
                stq = ss.on(nc.scalar.activation(xbf[:], conv[:], AF.Copy), 1)
                nc.scalar.wait_ge(ss.h, stq)
                stq = ss.on(nc.scalar.activation(sqb[:], xbf[:], AF.Square), 1)
                for s in range(4):
                    nc.tensor.wait_ge(ss.h, stq)
                    if s > 0 or True:
                        nc.tensor.wait_ge(vs.h, vs.v)
                    i = nc.tensor.matmul(pn[0:1, :], onesb[:], sqb[:, s * 512:(s + 1) * 512],
                                         start=True, stop=True)
                    ptv = pe.on(i, 1)
                    nc.scalar.wait_ge(pe.h, ptv)
                    nc.scalar.activation(lnrow[:], pn[0:1, :], AF.Ln,
                                         bias=eps_t[:])
                    if ti == 0:
                        i = nc.scalar.activation(lnrow[:], lnrow[:], AF.Exp, scale=mh[:], bias=lnb_q[:])
                    else:
                        i = nc.scalar.activation(lnrow[:], lnrow[:], AF.Exp, scale=mh[:])
                    st2 = ss.on(i, 1)
                    nc.vector.wait_ge(ss.h, st2)
                    vt2 = vs.on(nc.vector.tensor_copy(rstd_bf[:], lnrow[:]), 1)
                    nc.tensor.wait_ge(vs.h, vt2)
                    i = nc.tensor.matmul(pbc[:], onesrb[:], rstd_bf[:], start=True, stop=True)
                    ptv = pe.on(i, 1)
                    nc.vector.wait_ge(pe.h, ptv)
                    i = nc.vector.tensor_mul(outst[:, oslot * T + s * 512: oslot * T + (s + 1) * 512],
                                             conv[:, s * 512:(s + 1) * 512], pbc[:])
                    vt = vs.on(i, 1)
                nc.sync.wait_ge(vs.h, vt)
            osts.inc(oslot, nc.sync.dma_start(fm_view(out_d, fc, 1)[:, 0, :],
                                              outst[:, oslot * T:(oslot + 1) * T]))
            orot.done_at(*osts.pair(oslot))
    for sl_ in range(2):
        osts.wait(nc.sync, sl_)
        bsts.wait(nc.sync, sl_)



def emit_l1gate(env, ctx, xn_d, wg_d, normw_d, gate_d):
    nc = env.nc
    sb = lambda n, s, d=F32: ctx.enter_context(nc.sbuf_tensor(n, s, d))
    ld, wld, stm, vs, ss, pe = (env.sem("g1_" + x) for x in ["ld", "wld", "st", "v", "s", "pe"])

    xn_sb = sb("g1_xn", [128, NCH * T], BF16)
    wbuf = sb("g1_w", [128, 2 * NCH * 128], BF16)
    nw = sb("g1_nw", [128, 1])
    sg = sb("g1_sg", [128, 512])
    ostg = sb("g1_o", [128, 2 * T], BF16)
    pp = ctx.enter_context(nc.psum_tensor("g1_pp", [128, 2 * 512], F32))

    ld.on(nc.sync.dma_start(xn_sb[:].rearrange("p (c t) -> p c t", c=NCH), fm_view(xn_d)), 16)
    PL = ld.on(nc.sync.dma_start(nw[:], normw_d.ap()), 16)

    wlds = SlotSem(env, "g1_wlds", 2)
    osts = SlotSem(env, "g1_osts", 2)
    wrot = Rot(env, "g1_wrot", 2)
    orot = Rot(env, "g1_orot", 2)
    prot = Rot(env, "g1_prot", 2)
    for fc in range(NCH):
        ws = wrot.acquire(nc.gpsimd)
        wb0 = ws * NCH * 128
        wlds.inc(ws, nc.gpsimd.dma_start(
            wbuf[:, wb0:wb0 + NCH * 128].rearrange("p (c f) -> p c f", c=NCH),
            wg_d.ap().rearrange("(c p) f -> p c f", p=128)[:, :, fc * 128:(fc + 1) * 128]))
        oslot = orot.acquire(nc.scalar)
        for s in range(4):
            pslot = prot.acquire(nc.tensor)
            if fc == 0 and s == 0:
                nc.tensor.wait_ge(ld.h, PL)
            wlds.wait(nc.tensor, ws)
            for kc in range(NCH):
                i = nc.tensor.matmul(pp[:, pslot * 512:(pslot + 1) * 512],
                                     wbuf[:, wb0 + kc * 128: wb0 + (kc + 1) * 128],
                                     xn_sb[:, kc * T + s * 512: kc * T + (s + 1) * 512],
                                     start=(kc == 0), stop=(kc == NCH - 1))
            ptv = pe.on(i, 1)
            if s == 3:
                wrot.done_at(pe, ptv)
            nc.scalar.wait_ge(pe.h, ptv)
            stq = ss.on(nc.scalar.activation(sg[:], pp[:, pslot * 512:(pslot + 1) * 512], AF.Sigmoid), 1)
            nc.vector.wait_ge(ss.h, stq)
            if fc == 0 and s == 0:
                nc.vector.wait_ge(ld.h, PL)
            i = nc.vector.tensor_scalar(ostg[:, oslot * T + s * 512: oslot * T + (s + 1) * 512],
                                        sg[:], nw[:, 0:1], None, ALU.mult)
            vt = vs.on(i, 1)
            prot.done_at(vs, vt)
        nc.sync.wait_ge(vs.h, vt)
        osts.inc(oslot, nc.sync.dma_start(fm_view(gate_d, fc, 1)[:, 0, :], ostg[:, oslot * T:(oslot + 1) * T]))
        orot.done_at(*osts.pair(oslot))
    for sl_ in range(2):
        osts.wait(nc.sync, sl_)



def emit_l1delta(env, ctx, qn_d, kn_d, v_d, beta_d, gate_d, masksl_d, maskui_d,
                 maskbd_d, maskbdT_d, maskblnT_d, identbf_d, od_d):
    nc = env.nc
    NH = 16
    NLEV = 4
    NS = 4
    sb = lambda n, s, d=F32: ctx.enter_context(nc.sbuf_tensor(n, s, d))
    ld, stm, vs, ss, pe = (env.sem("d1_" + x) for x in ["ld", "st", "v", "s", "pe"])

    kc_sb = sb("d1_k", [128, 2 * NH * 128], BF16)
    qc_sb = sb("d1_q", [128, 2 * NH * 128], BF16)
    vc_sb = sb("d1_vv", [128, 2 * NH * 128], BF16)
    gc_sb = sb("d1_g", [128, 2 * NH * 128], BF16)
    bt_sb = sb("d1_b", [128, 2 * 16])
    S_sb = sb("d1_S", [128, NH * 128])
    Sbf = sb("d1_Sbf", [128, NH * 128], BF16)
    masksl = sb("d1_msl", [128, 128])
    maskui = sb("d1_mui", [128, 128])
    maskbd = sb("d1_mbd", [128, 128])
    maskbdT = sb("d1_mbdT", [128, 128])
    maskblnT = sb("d1_mblnT", [128, 128])
    identb = sb("d1_idb", [128, 128], BF16)
    identf32 = sb("d1_idf32", [128, 128])
    odst = sb("d1_od", [128, 2 * NH * 128], BF16)
    epsc = sb("d1_epsc", [128, 1])
    mhc = sb("d1_mh", [128, 1])
    i128 = sb("d1_i128", [128, 1])
    scr = []
    for j in range(NS):
        scr.append(dict(
            ch=[sb(f"d1_chA{j}", [128, 128], BF16), sb(f"d1_chB{j}", [128, 128], BF16)],
            chp=[sb(f"d1_chAp{j}", [128, 128], BF16), sb(f"d1_chBp{j}", [128, 128], BF16)],
            Pp=sb(f"d1_P{j}", [128, 128], BF16),
            vtm=sb(f"d1_vtm{j}", [128, 128], BF16),
            ktm=sb(f"d1_ktm{j}", [128, 128], BF16),
            Wsb=sb(f"d1_W{j}", [128, 128], BF16),
            Apr=sb(f"d1_A{j}", [128, 128], BF16),
            rhs32=sb(f"d1_rhs{j}", [128, 128]),
            sq32=sb(f"d1_sq{j}", [128, 128]),
            rstd=sb(f"d1_rstd{j}", [128, 4]),
            onbf=sb(f"d1_onbf{j}", [128, 128], BF16),
            Mpr=sb(f"d1_Mpr{j}", [128, 128], BF16),
            lnegT=sb(f"d1_lnegT{j}", [128, 128], BF16),
            zbf=sb(f"d1_zbf{j}", [128, 128], BF16),
            ybf=sb(f"d1_ybf{j}", [128, 128], BF16),
            Wacc=sb(f"d1_Wacc{j}", [128, 128]),
        ))

    pA = ctx.enter_context(nc.psum_tensor("d1_pA", [128, 2 * 512], F32))
    pO = ctx.enter_context(nc.psum_tensor("d1_pO", [128, 2 * 512], F32))
    pT0 = ctx.enter_context(nc.psum_tensor("d1_pT0", [128, 128], BF16))
    pT1 = ctx.enter_context(nc.psum_tensor("d1_pT1", [128, 128], BF16))

    nc.gpsimd.memset(epsc[:], 1e-5)
    nc.gpsimd.memset(mhc[:], -0.5)
    nc.gpsimd.memset(i128[:], 1.0 / 128.0)
    nc.gpsimd.memset(S_sb[:], 0.0)
    nc.gpsimd.memset(Sbf[:], 0.0)
    ld.on(nc.sync.dma_start(masksl[:], masksl_d.ap()), 16)
    ld.on(nc.sync.dma_start(maskui[:], maskui_d.ap()), 16)
    ld.on(nc.sync.dma_start(maskbd[:], maskbd_d.ap()), 16)
    ld.on(nc.sync.dma_start(maskbdT[:], maskbdT_d.ap()), 16)
    ld.on(nc.sync.dma_start(maskblnT[:], maskblnT_d.ap()), 16)
    PL = ld.on(nc.sync.dma_start(identb[:], identbf_d.ap()), 16)
    clds = SlotSem(env, "d1_clds", 2)
    odss = SlotSem(env, "d1_odss", 2)
    nc.vector.wait_ge(ld.h, PL)
    vt0 = vs.on(nc.vector.tensor_copy(identf32[:], identb[:]), 1)

    crot = Rot(env, "d1_crot", 2)
    orot = Rot(env, "d1_orot", 2)
    parot = Rot(env, "d1_parot", 2)
    porot = Rot(env, "d1_porot", 2)
    ptrot = [Rot(env, "d1_pt0", 1), Rot(env, "d1_pt1", 1)]

    def hsteps(c, cs, os_, h, S):
        cb = cs * NH * 128
        ob = os_ * NH * 128
        kq = cb + h * 128
        khs = kc_sb[:, kq:kq + 128]
        qhs = qc_sb[:, kq:kq + 128]
        vhs = vc_sb[:, kq:kq + 128]
        ghs = gc_sb[:, kq:kq + 128]
        bcol = bt_sb[:, cs * 16 + h: cs * 16 + h + 1]
        Ssl = S_sb[:, h * 128:(h + 1) * 128]
        Sbsl = Sbf[:, h * 128:(h + 1) * 128]
        ch, chp, Pp = S["ch"], S["chp"], S["Pp"]

        pa = parot.acquire(nc.tensor)
        if h == 0:
            clds.wait(nc.tensor, cs)
            if c == 0:
                nc.tensor.wait_ge(vs.h, vt0)
        i = nc.tensor.matmul(pA[:, pa * 512: pa * 512 + 128], khs, khs, start=True, stop=True)
        t1 = pe.on(i, 1)
        nc.vector.wait_ge(pe.h, t1)
        if h == 0:
            clds.wait(nc.vector, cs)
        i = nc.vector.scalar_tensor_tensor(S["Mpr"][:], pA[:, pa * 512: pa * 512 + 128], bcol,
                                           masksl[:], ALU.mult, ALU.mult)
        tM = vs.on(i, 1)
        parot.done_at(vs, tM)
        i = nc.vector.tensor_mul(ch[0][:], S["Mpr"][:], maskbd[:])
        vs.on(i, 1)
        pts = ptrot[0].acquire(nc.tensor)
        nc.tensor.wait_ge(vs.h, tM)
        i = nc.tensor.transpose(pT0[:], S["Mpr"][:], identb[:])
        t2 = pe.on(i, 1)
        nc.scalar.wait_ge(pe.h, t2)
        i = nc.scalar.activation(chp[0][:], pT0[:], AF.Copy)
        tMp = ss.on(i, 1)
        ptrot[0].done_at(ss, tMp)
        nc.vector.wait_ge(ss.h, tMp)
        nc.vector.tensor_mul(S["lnegT"][:], chp[0][:], maskblnT[:])
        i = nc.vector.tensor_mul(chp[0][:], chp[0][:], maskbdT[:])
        tMp = vs.on(i, 1)
        tP = vs.on(nc.vector.tensor_sub(Pp[:], identf32[:], chp[0][:]), 1)
        yield
        cur = 0
        prev_chp_t = None
        for lev in range(1, NLEV + 1):
            nxt = 1 - cur
            pa = parot.acquire(nc.tensor)
            if lev == 1:
                nc.tensor.wait_ge(vs.h, tP)
            else:
                nc.tensor.wait_ge(ss.h, prev_chp_t)
            i = nc.tensor.matmul(pA[:, pa * 512: pa * 512 + 128], chp[cur][:], ch[cur][:],
                                 start=True, stop=True)
            t3 = pe.on(i, 1)
            nc.scalar.wait_ge(pe.h, t3)
            i = nc.scalar.activation(ch[nxt][:], pA[:, pa * 512: pa * 512 + 128], AF.Copy)
            t5 = ss.on(i, 1)
            parot.done_at(ss, t5)
            if lev < NLEV:
                pa = parot.acquire(nc.tensor)
                i = nc.tensor.matmul(pA[:, pa * 512: pa * 512 + 128], ch[cur][:], chp[cur][:],
                                     start=True, stop=True)
                t4 = pe.on(i, 1)
                nc.scalar.wait_ge(pe.h, t4)
                i = nc.scalar.activation(chp[nxt][:], pA[:, pa * 512: pa * 512 + 128], AF.Copy)
                t5b = ss.on(i, 1)
                parot.done_at(ss, t5b)
                prev_chp_t = t5b
            po_ = porot.acquire(nc.tensor)
            nc.tensor.wait_ge(ss.h, t5)
            nc.tensor.wait_ge(vs.h, tP)
            i = nc.tensor.matmul(pO[:, po_ * 512: po_ * 512 + 128], ch[nxt][:], Pp[:],
                                 start=True, stop=True)
            t7 = pe.on(i, 1)
            nc.vector.wait_ge(pe.h, t7)
            i = nc.vector.tensor_add(Pp[:], Pp[:], pO[:, po_ * 512: po_ * 512 + 128])
            tP = vs.on(i, 1)
            porot.done_at(vs, tP)
            cur = nxt
            yield
        pts = ptrot[0].acquire(nc.tensor)
        i = nc.tensor.transpose(pT0[:], vhs, identb[:])
        t8 = pe.on(i, 1)
        nc.scalar.wait_ge(pe.h, t8)
        i = nc.scalar.activation(S["vtm"][:], pT0[:], AF.Copy)
        t9 = ss.on(i, 1)
        ptrot[0].done_at(ss, t9)
        pts = ptrot[1].acquire(nc.tensor)
        i = nc.tensor.transpose(pT1[:], khs, identb[:])
        t10 = pe.on(i, 1)
        nc.scalar.wait_ge(pe.h, t10)
        i = nc.scalar.activation(S["ktm"][:], pT1[:], AF.Copy)
        t11 = ss.on(i, 1)
        ptrot[1].done_at(ss, t11)
        po_ = porot.acquire(nc.tensor)
        i = nc.tensor.matmul(pO[:, po_ * 512: po_ * 512 + 128], khs, Sbsl, start=True, stop=True)
        t12 = pe.on(i, 1)
        nc.vector.wait_ge(pe.h, t12)
        nc.vector.wait_ge(ss.h, t9)
        i = nc.vector.tensor_sub(S["rhs32"][:], S["vtm"][:], pO[:, po_ * 512: po_ * 512 + 128])
        porot.done(i)
        i = nc.vector.tensor_scalar(S["Wsb"][:], S["rhs32"][:], bcol, None, ALU.mult)
        t13 = vs.on(i, 1)
        yield
        po_ = porot.acquire(nc.tensor)
        nc.tensor.wait_ge(vs.h, t13)
        i = nc.tensor.matmul(pO[:, po_ * 512: po_ * 512 + 128], Pp[:], S["Wsb"][:], start=True, stop=True)
        t14 = pe.on(i, 1)
        nc.scalar.wait_ge(pe.h, t14)
        i = nc.scalar.activation(S["zbf"][:], pO[:, po_ * 512: po_ * 512 + 128], AF.Copy)
        t15z = ss.on(i, 1)
        nc.vector.wait_ge(ss.h, t15z)
        i = nc.vector.tensor_copy(S["Wacc"][:], pO[:, po_ * 512: po_ * 512 + 128])
        tWa = vs.on(i, 1)
        porot.done_at(vs, tWa)
        for ksub in range(3):
            pa = parot.acquire(nc.tensor)
            nc.tensor.wait_ge(ss.h, t15z)
            i = nc.tensor.matmul(pA[:, pa * 512: pa * 512 + 128], S["lnegT"][:], S["zbf"][:],
                                 start=True, stop=True)
            ty = pe.on(i, 1)
            nc.scalar.wait_ge(pe.h, ty)
            i = nc.scalar.activation(S["ybf"][:], pA[:, pa * 512: pa * 512 + 128], AF.Copy)
            t15y = ss.on(i, 1)
            parot.done_at(ss, t15y)
            po_ = porot.acquire(nc.tensor)
            nc.tensor.wait_ge(ss.h, t15y)
            i = nc.tensor.matmul(pO[:, po_ * 512: po_ * 512 + 128], Pp[:], S["ybf"][:],
                                 start=True, stop=True)
            tz = pe.on(i, 1)
            nc.scalar.wait_ge(pe.h, tz)
            i = nc.scalar.activation(S["zbf"][:], pO[:, po_ * 512: po_ * 512 + 128], AF.Copy)
            t15z = ss.on(i, 1)
            nc.vector.wait_ge(ss.h, t15z)
            i = nc.vector.tensor_add(S["Wacc"][:], S["Wacc"][:], pO[:, po_ * 512: po_ * 512 + 128])
            tWa = vs.on(i, 1)
            porot.done_at(vs, tWa)
            yield
        i = nc.vector.tensor_copy(S["Wsb"][:], S["Wacc"][:])
        t15 = vs.on(i, 1)
        pa = parot.acquire(nc.tensor)
        i = nc.tensor.matmul(pA[:, pa * 512: pa * 512 + 128], khs, qhs, start=True, stop=True)
        t16 = pe.on(i, 1)
        nc.vector.wait_ge(pe.h, t16)
        i = nc.vector.tensor_mul(S["Apr"][:], pA[:, pa * 512: pa * 512 + 128], maskui[:])
        t17 = vs.on(i, 1)
        parot.done_at(vs, t17)
        po_ = porot.acquire(nc.tensor)
        nc.tensor.wait_ge(vs.h, t15)
        nc.tensor.wait_ge(vs.h, t17)
        nc.tensor.matmul(pO[:, po_ * 512: po_ * 512 + 128], qhs, Sbsl, start=True, stop=False)
        i = nc.tensor.matmul(pO[:, po_ * 512: po_ * 512 + 128], S["Apr"][:], S["Wsb"][:],
                             start=False, stop=True)
        t18 = pe.on(i, 1)
        pa = parot.acquire(nc.tensor)
        nc.tensor.wait_ge(ss.h, t11)
        i = nc.tensor.matmul(pA[:, pa * 512: pa * 512 + 128], S["ktm"][:], S["Wsb"][:],
                             start=True, stop=True)
        t19 = pe.on(i, 1)
        nc.vector.wait_ge(pe.h, t19)
        i = nc.vector.tensor_add(Ssl, Ssl, pA[:, pa * 512: pa * 512 + 128])
        parot.done(i)
        t20 = vs.on(nc.vector.tensor_copy(Sbsl, Ssl), 1)
        nc.scalar.wait_ge(pe.h, t18)
        i = nc.scalar.activation(S["sq32"][:], pO[:, po_ * 512: po_ * 512 + 128], AF.Square)
        t21 = ss.on(i, 1)
        nc.vector.wait_ge(ss.h, t21)
        t22 = vs.on(nc.vector.reduce_sum(S["rstd"][:, 0:1], S["sq32"][:], AX.X), 1)
        nc.scalar.wait_ge(vs.h, t22)
        nc.scalar.activation(S["rstd"][:, 1:2], S["rstd"][:, 0:1], AF.Ln, bias=epsc[:], scale=i128[:])
        t23 = ss.on(nc.scalar.activation(S["rstd"][:, 2:3], S["rstd"][:, 1:2], AF.Exp, scale=mhc[:]), 1)
        nc.vector.wait_ge(ss.h, t23)
        i = nc.vector.tensor_scalar(S["onbf"][:], pO[:, po_ * 512: po_ * 512 + 128],
                                    S["rstd"][:, 2:3], None, ALU.mult)
        t24 = vs.on(i, 1)
        porot.done_at(vs, t24)
        pts = ptrot[1].acquire(nc.tensor)
        nc.tensor.wait_ge(vs.h, t24)
        i = nc.tensor.transpose(pT1[:], S["onbf"][:], identb[:])
        t25 = pe.on(i, 1)
        nc.vector.wait_ge(pe.h, t25)
        i = nc.vector.tensor_mul(odst[:, ob + h * 128: ob + (h + 1) * 128], pT1[:], ghs)
        t26 = vs.on(i, 1)
        ptrot[1].done_at(vs, t26)
        S["last"] = t26

    for c in range(NTC):
        cs = crot.acquire(nc.sync)
        cb = cs * NH * 128
        clds.inc(cs, nc.sync.dma_start(qc_sb[:, cb:cb + NH * 128].rearrange("p (h t) -> p h t", h=NH),
                                       fm_view(qn_d, 0, NCH, c * 128, 128)))
        clds.inc(cs, nc.sync.dma_start(kc_sb[:, cb:cb + NH * 128].rearrange("p (h t) -> p h t", h=NH),
                                       fm_view(kn_d, 0, NCH, c * 128, 128)))
        clds.inc(cs, nc.sync.dma_start(vc_sb[:, cb:cb + NH * 128].rearrange("p (h t) -> p h t", h=NH),
                                       fm_view(v_d, 0, NCH, c * 128, 128)))
        clds.inc(cs, nc.sync.dma_start(gc_sb[:, cb:cb + NH * 128].rearrange("p (h t) -> p h t", h=NH),
                                       fm_view(gate_d, 0, NCH, c * 128, 128)))
        clds.inc(cs, nc.sync.dma_start(bt_sb[:, cs * 16:(cs + 1) * 16],
                                       beta_d.ap()[c * 128:(c + 1) * 128, :]))
        os_ = orot.acquire(nc.vector)
        ob = os_ * NH * 128
        last_t = None
        for hg in range(0, NH, NS):
            gens = [hsteps(c, cs, os_, hg + j, scr[j]) for j in range(NS)]
            alive = list(gens)
            while alive:
                for ggen in list(alive):
                    try:
                        next(ggen)
                    except StopIteration:
                        alive.remove(ggen)
            for j in range(NS):
                last_t = max(last_t or 0, scr[j]["last"])
        nc.sync.wait_ge(vs.h, last_t)
        odss.inc(os_, nc.sync.dma_start(fm_view(od_d, 0, NCH, c * 128, 128),
                                        odst[:, ob:ob + NH * 128].rearrange("p (h t) -> p h t", h=NH)))
        orot.done_at(*odss.pair(os_))
        if c == NTC - 1:
            pass
        crot.done_at(*odss.pair(os_))
    for sl_ in range(2):
        odss.wait(nc.sync, sl_)



def emit_l1ffn(env, ctx, xn_d, gw_d, uw_d, dw_d, gs_d, us_d, ds_d, x2a_d, xout_d):
    nc = env.nc
    II = 8192
    NIC = II // 128
    sb = lambda n, s, d=F32: ctx.enter_context(nc.sbuf_tensor(n, s, d))
    ld, wld, dld, stm, vs, ss, pe = (env.sem("f1_" + x) for x in
                                     ["ld", "wld", "dld", "st", "v", "s", "pe"])
    HT = 1024
    hstash = sb("f1_h", [128, NIC * HT], BF16)
    gsc = sb("f1_gs", [128, 64])
    usc = sb("f1_us", [128, 64])
    dsc = sb("f1_ds", [128, 16])
    onesb = sb("f1_onesb", [128, 1], BF16)
    onesrf = sb("f1_onesrf", [1, 128])

    nc.gpsimd.memset(onesb[:], 1.0)
    nc.gpsimd.memset(onesrf[:], 1.0)
    ld.on(nc.sync.dma_start(gsc[:], gs_d.ap()), 16)
    ld.on(nc.sync.dma_start(usc[:], us_d.ap()), 16)
    PL = ld.on(nc.sync.dma_start(dsc[:], ds_d.ap()), 16)
    wlds = SlotSem(env, "f1_wlds", 2)
    dlds = SlotSem(env, "f1_dlds", 2)
    xlds = SlotSem(env, "f1_xlds", 2)
    xres_l = SlotSem(env, "f1_xres", 2)
    xsts = SlotSem(env, "f1_xsts", 2)

    wrot = Rot(env, "f1_wrot", 2)
    drot = Rot(env, "f1_drot", 2)
    pgrot = Rot(env, "f1_pgrot", 2)
    pdrot = Rot(env, "f1_pdrot", 2)

    for half in range(2):
        t0 = half * HT
        with ExitStack() as c2:
            xn_sb = c2.enter_context(nc.sbuf_tensor("f1_xn%d" % half, [128, NCH * HT], BF16))
            gwb = c2.enter_context(nc.sbuf_tensor("f1_gw%d" % half, [128, 2 * NCH * 256], BF16))
            gsil = c2.enter_context(nc.sbuf_tensor("f1_gsil%d" % half, [128, 2 * 512], F32))
            ucs = c2.enter_context(nc.sbuf_tensor("f1_uc%d" % half, [128, 2 * 512], F32))
            pg = c2.enter_context(nc.psum_tensor("f1_pg%d" % half, [128, 2 * 512], F32))
            pu = c2.enter_context(nc.psum_tensor("f1_pu%d" % half, [128, 2 * 512], F32))
            if half == 1:
                xsts.wait(nc.sync, 0)
                xsts.wait(nc.gpsimd, 0)
                nc.sync.wait_ge(pe.h, pe.v)
            xlds.inc(half, nc.sync.dma_start(xn_sb[:].rearrange("p (c t) -> p c t", c=NCH),
                                             fm_view(xn_d, 0, NCH, t0, HT)))
            csx = c2.enter_context(nc.sbuf_tensor("f1_csx%d" % half, [1, 2 * 512], F32))
            pxs = c2.enter_context(nc.psum_tensor("f1_pxs%d" % half, [2, 2 * 512], F32))
            xlds.wait(nc.tensor, half)
            for s2 in range(2):
                for kc in range(NCH):
                    i = nc.tensor.matmul(pxs[0:1, s2 * 512:(s2 + 1) * 512], onesb[:],
                                         xn_sb[:, kc * HT + s2 * 512: kc * HT + (s2 + 1) * 512],
                                         start=(kc == 0), stop=(kc == NCH - 1), skip_group_check=True)
            ptv = pe.on(i, 1)
            nc.vector.wait_ge(pe.h, ptv)
            xsv = vs.on(nc.vector.tensor_scalar(csx[:], pxs[0:1, :], -128.0, None, ALU.mult), 1)
            for ic in range(NIC):
                ws = wrot.acquire(nc.gpsimd)
                base = ws * NCH * 256
                wlds.inc(ws, nc.gpsimd.dma_start(
                    gwb[:, base:base + NCH * 256].rearrange("p (c f) -> p c f", c=NCH)[:, :, 0:128],
                    gw_d.ap().rearrange("(c p) f -> p c f", p=128)[:, :, ic * 128:(ic + 1) * 128]))
                wlds.inc(ws, nc.gpsimd.dma_start(
                    gwb[:, base:base + NCH * 256].rearrange("p (c f) -> p c f", c=NCH)[:, :, 128:256],
                    uw_d.ap().rearrange("(c p) f -> p c f", p=128)[:, :, ic * 128:(ic + 1) * 128]))
                for s2 in range(2):
                    pgs = pgrot.acquire(nc.tensor)
                    if ic == 0 and s2 == 0:
                        nc.tensor.wait_ge(vs.h, xsv)
                    wlds.wait(nc.tensor, ws)
                    for kc in range(NCH):
                        nc.tensor.matmul(pg[:, pgs * 512:(pgs + 1) * 512],
                                         gwb[:, base + kc * 256: base + kc * 256 + 128],
                                         xn_sb[:, kc * HT + s2 * 512: kc * HT + (s2 + 1) * 512],
                                         start=(kc == 0), stop=False)
                    nc.tensor.matmul(pg[:, pgs * 512:(pgs + 1) * 512], onesrf[:],
                                     csx[0:1, s2 * 512:(s2 + 1) * 512],
                                     start=False, stop=True, skip_group_check=True)
                    for kc in range(NCH):
                        nc.tensor.matmul(pu[:, pgs * 512:(pgs + 1) * 512],
                                         gwb[:, base + kc * 256 + 128: base + kc * 256 + 256],
                                         xn_sb[:, kc * HT + s2 * 512: kc * HT + (s2 + 1) * 512],
                                         start=(kc == 0), stop=False)
                    i = nc.tensor.matmul(pu[:, pgs * 512:(pgs + 1) * 512], onesrf[:],
                                         csx[0:1, s2 * 512:(s2 + 1) * 512],
                                         start=False, stop=True, skip_group_check=True)
                    ptv = pe.on(i, 1)
                    if s2 == 1:
                        wrot.done_at(pe, ptv)
                    nc.scalar.wait_ge(pe.h, ptv)
                    if ic == 0 and s2 == 0 and half == 0:
                        nc.scalar.wait_ge(ld.h, PL)
                    stq = ss.on(nc.scalar.activation(gsil[:, pgs * 512:(pgs + 1) * 512],
                                                     pg[:, pgs * 512:(pgs + 1) * 512], AF.Silu,
                                                     scale=gsc[:, ic // 4 * 4 + ic % 4: ic // 4 * 4 + ic % 4 + 1]
                                                     if False else gsc[:, ic:ic + 1]), 1)
                    nc.vector.wait_ge(ss.h, stq)
                    nc.vector.tensor_scalar(ucs[:, pgs * 512:(pgs + 1) * 512],
                                            pu[:, pgs * 512:(pgs + 1) * 512],
                                            usc[:, ic:ic + 1], None, ALU.mult)
                    i = nc.vector.tensor_mul(hstash[:, ic * HT + s2 * 512: ic * HT + (s2 + 1) * 512],
                                             gsil[:, pgs * 512:(pgs + 1) * 512],
                                             ucs[:, pgs * 512:(pgs + 1) * 512])
                    hv = vs.on(i, 1)
                    pgrot.done_at(vs, hv)
        with ExitStack() as c2:
            xacc = c2.enter_context(nc.sbuf_tensor("f1_xa%d" % half, [128, 2 * HT], F32))
            dwb = c2.enter_context(nc.sbuf_tensor("f1_dw%d" % half, [128, 2 * 2048], BF16))
            csrow = c2.enter_context(nc.sbuf_tensor("f1_cs%d" % half, [1, 8 * 512], F32))
            pcs = c2.enter_context(nc.psum_tensor("f1_pcs%d" % half, [2, 2 * 512], F32))
            pd = c2.enter_context(nc.psum_tensor("f1_pd%d" % half, [128, 2 * 512], F32))
            nc.sync.wait_ge(pe.h, pe.v)
            nc.sync.wait_ge(vs.h, vs.v)
            nc.gpsimd.wait_ge(pe.h, pe.v)
            nc.gpsimd.wait_ge(vs.h, vs.v)
            csv = vs.v
            for sblk in range(4):
                if sblk > 0:
                    nc.tensor.wait_ge(vs.h, csv)
                for s2 in range(2):
                    for icb in range(16):
                        ic = sblk * 16 + icb
                        i = nc.tensor.matmul(pcs[0:1, s2 * 512:(s2 + 1) * 512], onesb[:],
                                             hstash[:, ic * HT + s2 * 512: ic * HT + (s2 + 1) * 512],
                                             start=(icb == 0), stop=(icb == 15), skip_group_check=True)
                ptv = pe.on(i, 1)
                nc.vector.wait_ge(pe.h, ptv)
                i = nc.vector.tensor_scalar(csrow[:, sblk * 1024:(sblk + 1) * 1024],
                                            pcs[0:1, :], -128.0, None, ALU.mult)
                csv = vs.on(i, 1)
            nc.tensor.wait_ge(vs.h, csv)
            for fc in range(NCH):
                xs = fc % 2
                if fc >= 2 or half == 1:
                    xsts.wait(nc.sync, xs)
                xres_l.inc(xs, nc.sync.dma_start(xacc[:, xs * HT:(xs + 1) * HT],
                                                 fm_view(x2a_d, fc, 1, t0, HT)[:, 0, :]))
                pds = [pdrot.acquire(nc.tensor) for _ in range(2)]
                for sblk in range(4):
                    dsl = drot.acquire(nc.gpsimd)
                    dlds.inc(dsl, nc.gpsimd.dma_start(
                        dwb[:, dsl * 2048:(dsl + 1) * 2048].rearrange("p (c f) -> p c f", c=16),
                        dw_d.ap().rearrange("(c p) f -> p c f", p=128)[:, sblk * 16:(sblk + 1) * 16,
                                                                      fc * 128:(fc + 1) * 128]))
                    dlds.wait(nc.tensor, dsl)
                    for s2 in range(2):
                        for icb in range(16):
                            ic = sblk * 16 + icb
                            iw = nc.tensor.matmul(pd[:, pds[s2] * 512:(pds[s2] + 1) * 512],
                                                  dwb[:, dsl * 2048 + icb * 128: dsl * 2048 + (icb + 1) * 128],
                                                  hstash[:, ic * HT + s2 * 512: ic * HT + (s2 + 1) * 512],
                                                  start=(sblk == 0 and icb == 0), stop=False,
                                                  skip_group_check=True)
                        r0 = (sblk * 2 + s2) * 512
                        i = nc.tensor.matmul(pd[:, pds[s2] * 512:(pds[s2] + 1) * 512], onesrf[:],
                                             csrow[0:1, r0:r0 + 512],
                                             start=False, stop=(sblk == 3), skip_group_check=True)
                    drot.done(iw)
                ptv = pe.on(i, 1)
                for s2 in range(2):
                    nc.vector.wait_ge(pe.h, ptv)
                    xres_l.wait(nc.vector, xs)
                    i = nc.vector.scalar_tensor_tensor(xacc[:, xs * HT + s2 * 512: xs * HT + (s2 + 1) * 512],
                                                       pd[:, pds[s2] * 512:(pds[s2] + 1) * 512],
                                                       dsc[:, fc:fc + 1],
                                                       xacc[:, xs * HT + s2 * 512: xs * HT + (s2 + 1) * 512],
                                                       ALU.mult, ALU.add)
                    av = vs.on(i, 1)
                    pdrot.done_at(vs, av)
                nc.sync.wait_ge(vs.h, av)
                xsts.inc(xs, nc.sync.dma_start(fm_view(xout_d, fc, 1, t0, HT)[:, 0, :],
                                               xacc[:, xs * HT:(xs + 1) * HT]))
            for sl_ in range(2):
                xsts.wait(nc.sync, sl_)
                xsts.wait(nc.gpsimd, sl_)



SH = {
    "x0": ([D, T], F32), "xn0": ([D, T], BF16), "q": ([D, T], BF16),
    "k": ([512, T], BF16), "vaug": ([T, 8, 65], BF16), "oall": ([D, T], BF16),
    "x1a": ([D, T], F32), "x1": ([D, T], F32), "xn1": ([D, T], BF16),
    "qn": ([D, T], BF16), "kn": ([D, T], BF16), "v1": ([D, T], BF16),
    "beta": ([T, 16], F32), "gate": ([D, T], BF16), "od": ([D, T], BF16),
    "x2a": ([D, T], F32), "xout": ([D, T], F32),
    "l0_wqkv_t": ([D, 3072], F32), "l0_bqk": ([128, 20], F32), "l0_bv_bc": ([128, 512], F32),
    "l0_wout_t": ([D, D], F32), "l0_bout": ([128, 16], F32),
    "l0_router_t": ([D, 16], F32), "l0_ebias_bc": ([128, 16], F32),
    "selmat": ([16, 2048], F32), "identf": ([128, 128], F32), "identbf": ([128, 128], BF16),
    "mask_sl": ([128, 128], F32), "mask_ui": ([128, 128], F32),
    "mask_bd": ([128, 128], F32), "mask_bdT": ([128, 128], F32), "mask_blnegT": ([128, 128], F32),
    "l0_gup_t": ([17, D, 1536], F32), "l0_dwn_t": ([17, 768, D], F32),
    "l1_wq": ([D, D], F32), "l1_wk": ([D, D], F32), "l1_wv": ([D, D], F32),
    "l1_wo": ([D, D], F32), "l1_wg": ([D, D], F32), "l1_wb": ([D, 16], F32),
    "l1_cwq": ([128, 64], F32), "l1_cwk": ([128, 64], F32), "l1_cwv": ([128, 64], F32),
    "l1_normw": ([128, 1], F32),
    "l1_gw_t": ([D, 8192], I32), "l1_uw_t": ([D, 8192], I32), "l1_dw_t": ([8192, D], I32),
    "l1_gs": ([128, 64], F32), "l1_us": ([128, 64], F32), "l1_ds": ([128, 16], F32),
}

PHASES = [
    ("ln0", emit_ln, ["x0"], ["xn0"]),
    ("l0qkv", emit_l0qkv, ["xn0", "l0_wqkv_t", "l0_bqk", "l0_bv_bc"], ["q", "k", "vaug"]),
    ("l0att", emit_l0att, ["q", "k", "vaug"], ["oall"]),
    ("l0wo", emit_l0wo, ["oall", "x0", "l0_wout_t", "l0_bout"], ["x1a"]),
    ("l0moe", emit_l0moe, ["xn0", "x1a", "l0_router_t", "l0_ebias_bc", "selmat", "identf",
                           "l0_gup_t", "l0_dwn_t"], ["x1"]),
    ("ln1", emit_ln, ["x1"], ["xn1"]),
    ("l1proj", emit_l1proj, ["xn1", "l1_wq", "l1_wk", "l1_wv", "l1_wb",
                             "l1_cwq", "l1_cwk", "l1_cwv"], ["qn", "kn", "v1", "beta"]),
    ("l1gate", emit_l1gate, ["xn1", "l1_wg", "l1_normw"], ["gate"]),
    ("l1delta", emit_l1delta, ["qn", "kn", "v1", "beta", "gate", "mask_sl", "mask_ui",
                               "mask_bd", "mask_bdT", "mask_blnegT", "identbf"], ["od"]),
    ("l1wo", emit_l1wo, ["od", "x1", "l1_wo"], ["x2a"]),
    ("l1ffn", emit_l1ffn, ["xn1", "l1_gw_t", "l1_uw_t", "l1_dw_t", "l1_gs", "l1_us", "l1_ds",
                           "x2a"], ["xout"]),
]


INTERMEDIATES = ["xn0", "q", "k", "vaug", "oall", "x1a", "x1", "xn1",
                 "qn", "kn", "v1", "beta", "gate", "od", "x2a"]


class NCProxy:

    def __init__(self, nc, tag):
        object.__setattr__(self, "_nc", nc)
        object.__setattr__(self, "_tag", tag)

    def sbuf_tensor(self, name, *a, **k):
        return self._nc.sbuf_tensor(f"{name}__{self._tag}", *a, **k)

    def psum_tensor(self, name, *a, **k):
        return self._nc.psum_tensor(f"{name}__{self._tag}", *a, **k)

    def __getattr__(self, a):
        return getattr(self._nc, a)


def build_fused():
    nc = bass.Bass("TRN2", target_bir_lowering=False)
    dram = {}
    innames = set()
    for name, fn, ins, outs in PHASES:
        for n in ins + outs:
            if n in dram:
                continue
            shape, dt = SH[n]
            if n in INTERMEDIATES:
                dram[n] = nc.dram_tensor(n, list(shape), dt)
            elif n == "xout":
                dram[n] = nc.dram_tensor(n, list(shape), dt, kind="ExternalOutput")
            else:
                dram[n] = nc.dram_tensor(n, list(shape), dt, kind="ExternalInput")
                innames.add(n)
    with ExitStack() as ctx:
        env = Env(nc, ctx)
        for k, (name, fn, ins, outs) in enumerate(PHASES):
            env.nc = NCProxy(nc, f"{k}")
            with nc.named_scope(name):
                with ExitStack() as pctx:
                    fn(env, pctx, *[dram[n] for n in ins], *[dram[n] for n in outs])
                nc.all_engine_barrier()
        env.nc = nc
    return nc, sorted(innames)


def build_phase(name):
    for nm, fn, ins, outs in PHASES:
        if nm == name:
            break
    else:
        raise KeyError(name)
    nc = bass.Bass("TRN2", target_bir_lowering=False)
    dins = [nc.dram_tensor(n, list(SH[n][0]), SH[n][1], kind="ExternalInput") for n in ins]
    douts = [nc.dram_tensor(n, list(SH[n][0]), SH[n][1], kind="ExternalOutput") for n in outs]
    with ExitStack() as ctx:
        env = Env(nc, ctx)
        fn(env, ctx, *dins, *douts)
    return nc, ins, outs


def kernel(**inputs):
    st = stage(inputs)
    nc, innames = build_fused()
    in_maps = [{n: np.ascontiguousarray(st[n][c] if n == "x0" else st[n]) for n in innames}
               for c in range(NCORE)]
    res = run_bass_kernel_spmd(nc, in_maps, core_ids=list(range(NCORE)))
    return np.stack([np.asarray(res.results[c]["xout"], np.float32).T for c in range(NCORE)])


def kernel_unfused(**inputs):
    st = stage(inputs)
    percore = {"x0": st["x0"]}

    def fetch(n, c):
        if n in percore:
            return percore[n][c]
        return st[n]

    for name, _, _, _ in PHASES:
        nc, innames, outnames = build_phase(name)
        in_maps = [{n: np.ascontiguousarray(fetch(n, c)) for n in innames} for c in range(NCORE)]
        res = run_bass_kernel_spmd(nc, in_maps, core_ids=list(range(NCORE)))
        for n in outnames:
            percore[n] = [res.results[c][n] for c in range(NCORE)]
    return np.stack([np.asarray(percore["xout"][c], np.float32).T for c in range(NCORE)])


# version 14 (final)
# speedup vs baseline: 1.0174x; 1.0174x over previous
import numpy as np
import ml_dtypes
from contextlib import ExitStack

import concourse.bass as bass
import concourse.mybir as mybir
from concourse.bass_utils import run_bass_kernel_spmd

F32 = mybir.dt.float32
BF16 = mybir.dt.bfloat16
I32 = mybir.dt.int32
AF = mybir.ActivationFunctionType
ALU = mybir.AluOpType
AX = mybir.AxisListType
BF = ml_dtypes.bfloat16

D = 2048
T = 2048
NCH = D // 128
NTC = T // 128
NCORE = 8



class Sem:
    def __init__(self, h):
        self.h = h
        self.v = 0

    def on(self, instr, k=1):
        instr.then_inc(self.h, k)
        self.v += k
        return self.v


class Env:
    def __init__(self, nc, ctx):
        self.nc = nc
        self.ctx = ctx
        self.sems = {}

    PREFIXES = ("ln_", "qk_", "at_", "wo_", "w1_", "mo_", "p1_", "g1_", "d1_", "f1_")

    def sem(self, name):
        for p in self.PREFIXES:
            if name.startswith(p):
                name = name[len(p):]
                break
        if name not in self.sems:
            self.sems[name] = Sem(self.ctx.enter_context(self.nc.semaphore(name)))
        return self.sems[name]


class Rot:

    def __init__(self, env, name, nslots):
        self.sem = env.sem(name)
        self.n = nslots
        self.uses = 0
        self.waits = []

    def acquire(self, *engs):
        u = self.uses
        self.uses += 1
        if u >= self.n:
            s, v = self.waits[u - self.n]
            for eng in engs:
                eng.wait_ge(s, v)
        return u % self.n

    def done(self, instr):
        v = self.sem.on(instr, 1)
        self.waits.append((self.sem.h, v))

    def done_at(self, sem, v):
        self.waits.append((sem.h, v))


class SlotSem:

    def __init__(self, env, name, n):
        self.sems = [env.sem(f"{name}{i}") for i in range(n)]
        self.tgt = [0] * n
        self.n = n

    def inc(self, slot, instr):
        sem = self.sems[slot]
        sem.on(instr, 16)
        self.tgt[slot] = sem.v

    def wait(self, eng, slot):
        eng.wait_ge(self.sems[slot].h, self.tgt[slot])

    def pair(self, slot):
        return (self.sems[slot], self.tgt[slot])


def fm_view(dram, c0=0, ncols=None, t0=0, nt=None):
    ap = dram.ap() if hasattr(dram, "ap") else dram
    F = ap.shape[0]
    nchunk = F // 128
    v = ap.rearrange("(c p) t -> p c t", p=128)
    ncols = nchunk if ncols is None else ncols
    nt = v.shape[2] - t0 if nt is None else nt
    return v[:, c0:c0 + ncols, t0:t0 + nt]



def _cols(v):
    return np.ascontiguousarray(np.asarray(v, np.float32).reshape(-1, 128).T)


def stage(inputs):
    st = {}
    x0 = np.asarray(inputs["L0_attn_x"], np.float32)
    st["x0"] = [np.ascontiguousarray(x0[i].T) for i in range(NCORE)]

    st["l0_wqkv_t"] = np.ascontiguousarray(np.asarray(inputs["L0_attn_w_qkv"], np.float32).T)
    st["l0_bqk"] = _cols(np.asarray(inputs["L0_attn_b_qkv"], np.float32)[:2560])
    bv = np.asarray(inputs["L0_attn_b_qkv"], np.float32)[2560:]
    st["l0_bv_bc"] = np.ascontiguousarray(np.broadcast_to(bv[None, :], (128, 512)))
    st["l0_wout_t"] = np.ascontiguousarray(np.asarray(inputs["L0_attn_w_out"], np.float32).T)
    st["l0_bout"] = _cols(inputs["L0_attn_b_out"])

    st["l0_router_t"] = np.ascontiguousarray(np.asarray(inputs["L0_ffn_router_w"], np.float32).T)
    st["l0_ebias_bc"] = np.ascontiguousarray(
        np.broadcast_to(np.asarray(inputs["L0_ffn_e_bias"], np.float32)[None, :], (128, 16)))
    gup = np.asarray(inputs["L0_ffn_gate_up_proj"], np.float32)
    dwn = np.asarray(inputs["L0_ffn_down_proj"], np.float32)
    sg = np.asarray(inputs["L0_ffn_shared_gate_w"], np.float32)
    su = np.asarray(inputs["L0_ffn_shared_up_w"], np.float32)
    sd = np.asarray(inputs["L0_ffn_shared_down_w"], np.float32)
    st["l0_gup_t"] = np.ascontiguousarray(np.concatenate(
        [gup.transpose(0, 2, 1), np.concatenate([sg, su], axis=0).T[None]], axis=0))
    st["l0_dwn_t"] = np.ascontiguousarray(np.concatenate(
        [dwn.transpose(0, 2, 1), sd.T[None]], axis=0))

    for nm, key in [("l1_wq", "L1_attn_Wq"), ("l1_wk", "L1_attn_Wk"), ("l1_wv", "L1_attn_Wv"),
                    ("l1_wo", "L1_attn_Wo"), ("l1_wg", "L1_attn_Wg"), ("l1_wb", "L1_attn_Wb")]:
        st[nm] = np.ascontiguousarray(np.asarray(inputs[key], np.float32))
    for nm, key in [("l1_cwq", "L1_attn_conv_wq"), ("l1_cwk", "L1_attn_conv_wk"),
                    ("l1_cwv", "L1_attn_conv_wv")]:
        w = np.asarray(inputs[key], np.float32).reshape(16, 128, 4).transpose(1, 0, 2)
        st[nm] = np.ascontiguousarray(w.reshape(128, 64))
    st["l1_normw"] = np.ascontiguousarray(np.asarray(inputs["L1_attn_norm_w"], np.float32)[:, None])

    st["l1_gw_t"] = np.ascontiguousarray(np.asarray(inputs["L1_ffn_gate_w"], np.int32).T)
    st["l1_uw_t"] = np.ascontiguousarray(np.asarray(inputs["L1_ffn_up_w"], np.int32).T)
    st["l1_dw_t"] = np.ascontiguousarray(np.asarray(inputs["L1_ffn_down_w"], np.int32).T)
    st["l1_gs"] = _cols(inputs["L1_ffn_gate_scale"])
    st["l1_us"] = _cols(inputs["L1_ffn_up_scale"])
    st["l1_ds"] = _cols(inputs["L1_ffn_down_scale"])

    st["identbf"] = np.eye(128, dtype=BF)
    st["identf"] = np.eye(128, dtype=np.float32)
    st["mask_sl"] = np.ascontiguousarray(np.tril(np.ones((128, 128), np.float32), -1))
    st["mask_ui"] = np.ascontiguousarray(np.triu(np.ones((128, 128), np.float32), 0))
    bdm = np.zeros((128, 128), np.float32)
    for bb_ in range(4):
        bdm[bb_ * 32:(bb_ + 1) * 32, bb_ * 32:(bb_ + 1) * 32] = np.tril(np.ones((32, 32), np.float32), -1)
    blm = st["mask_sl"] - bdm
    st["mask_bd"] = np.ascontiguousarray(bdm)
    st["mask_bdT"] = np.ascontiguousarray(bdm.T)
    st["mask_blnegT"] = np.ascontiguousarray(-blm.T)
    sel = np.zeros((16, 16 * 128), np.float32)
    for e in range(16):
        sel[e, e * 128:(e + 1) * 128] = 1.0
    st["selmat"] = sel
    return st



def emit_ln(env, ctx, x_d, xn_d):
    nc = env.nc
    sb = lambda n, s, d=F32: ctx.enter_context(nc.sbuf_tensor(n, s, d))
    ps = lambda n, s, d=F32: ctx.enter_context(nc.psum_tensor(n, s, d))
    ld, stm, vs, ss, pe = (env.sem("ln_" + x) for x in ["ld", "st", "v", "s", "pe"])

    xt = sb("ln_xt", [128, 2 * T])
    xbf = sb("ln_xbf", [128, 2 * T], BF16)
    sqbf = sb("ln_sqbf", [128, 2 * T], BF16)
    onesb = sb("ln_onesb", [128, 1], BF16)
    onesr = sb("ln_onesr", [1, 128])
    rows = sb("ln_rows", [1, 5 * T])
    Br = sb("ln_Br", [128, T])
    Bmr = sb("ln_Bmr", [128, T])
    xnst = sb("ln_xnst", [128, 2 * T], BF16)
    tmp32 = sb("ln_tmp32", [128, T])
    eps_t = sb("ln_eps", [1, 1])
    mh = sb("ln_mh", [1, 1])

    nc.gpsimd.memset(onesb[:], 1.0)
    nc.gpsimd.memset(onesr[:], 1.0)
    nc.gpsimd.memset(eps_t[:], 1e-5)
    c0 = pe.on(nc.gpsimd.memset(mh[:], -0.5), 1)
    lds = SlotSem(env, "ln_lds", 2)

    M0, V0, R0, MR0, T0 = 0, T, 2 * T, 3 * T, 4 * T
    with ExitStack() as c2:
        s1p = c2.enter_context(nc.psum_tensor("ln_s1", [2, T], F32))
        s2p = c2.enter_context(nc.psum_tensor("ln_s2", [2, T], F32))
        vs0 = vs.v
        for c in range(NCH):
            b = c % 2
            if c >= 2:
                nc.sync.wait_ge(vs.h, vs0 + c - 1)
            lds.inc(b, nc.sync.dma_start(xt[:, b * T:(b + 1) * T], fm_view(x_d, c, 1)[:, 0, :]))
            if c >= 2:
                nc.vector.wait_ge(pe.h, c0 + (c - 1))
            lds.wait(nc.vector, b)
            vt = vs.on(nc.vector.tensor_copy(xbf[:, b * T:(b + 1) * T], xt[:, b * T:(b + 1) * T]), 1)
            nc.scalar.wait_ge(vs.h, vt)
            stq = ss.on(nc.scalar.activation(sqbf[:, b * T:(b + 1) * T], xbf[:, b * T:(b + 1) * T], AF.Square), 1)
            nc.tensor.wait_ge(ss.h, stq)
            for s in range(4):
                nc.tensor.matmul(s1p[0:1, s * 512:(s + 1) * 512], onesb[:],
                                 xbf[:, b * T + s * 512: b * T + (s + 1) * 512],
                                 start=(c == 0), stop=(c == NCH - 1), skip_group_check=True)
            for s in range(4):
                i = nc.tensor.matmul(s2p[0:1, s * 512:(s + 1) * 512], onesb[:],
                                     sqbf[:, b * T + s * 512: b * T + (s + 1) * 512],
                                     start=(c == 0), stop=(c == NCH - 1), skip_group_check=True)
            pe.on(i, 1)
        nc.vector.wait_ge(pe.h, pe.v)
        nc.vector.tensor_scalar(rows[:, M0:M0 + T], s1p[0:1, :], 1.0 / D, None, ALU.mult)
        nc.vector.tensor_mul(rows[:, T0:T0 + T], rows[:, M0:M0 + T], rows[:, M0:M0 + T])
        vt = vs.on(nc.vector.scalar_tensor_tensor(rows[:, V0:V0 + T], s2p[0:1, :], 1.0 / D,
                                                  rows[:, T0:T0 + T], ALU.mult, ALU.subtract), 1)
        nc.scalar.wait_ge(vs.h, vt)
        nc.scalar.activation(rows[:, T0:T0 + T], rows[:, V0:V0 + T], AF.Ln, bias=eps_t[:])
        stq = ss.on(nc.scalar.activation(rows[:, R0:R0 + T], rows[:, T0:T0 + T], AF.Exp, scale=mh[:]), 1)
        nc.vector.wait_ge(ss.h, stq)
        vt = vs.on(nc.vector.tensor_mul(rows[:, MR0:MR0 + T], rows[:, M0:M0 + T], rows[:, R0:R0 + T]), 1)
        nc.tensor.wait_ge(vs.h, vt)
    with ExitStack() as c2:
        brp = c2.enter_context(nc.psum_tensor("ln_brp", [128, T], F32))
        bmrp = c2.enter_context(nc.psum_tensor("ln_bmrp", [128, T], F32))
        for s in range(4):
            nc.tensor.matmul(brp[:, s * 512:(s + 1) * 512], onesr[:],
                             rows[0:1, R0 + s * 512:R0 + (s + 1) * 512], start=True, stop=True)
        for s in range(4):
            i = nc.tensor.matmul(bmrp[:, s * 512:(s + 1) * 512], onesr[:],
                                 rows[0:1, MR0 + s * 512:MR0 + (s + 1) * 512], start=True, stop=True)
        bc = pe.on(i, 1)
        nc.vector.wait_ge(pe.h, bc)
        nc.vector.tensor_copy(Br[:], brp[:])
        vt = vs.on(nc.vector.tensor_copy(Bmr[:], bmrp[:]), 1)
        nc.tensor.wait_ge(vs.h, vt)

    vbase = vs.v
    sts = SlotSem(env, "ln_sts", 2)
    for c in range(NCH):
        b = c % 2
        if c >= 2:
            nc.sync.wait_ge(vs.h, vbase + 2 * (c - 1))
        lds.inc(b, nc.sync.dma_start(xt[:, b * T:(b + 1) * T], fm_view(x_d, c, 1)[:, 0, :]))
        if c >= 2:
            sts.wait(nc.vector, b)
        lds.wait(nc.vector, b)
        vs.on(nc.vector.tensor_mul(tmp32[:], xt[:, b * T:(b + 1) * T], Br[:]), 1)
        vt = vs.on(nc.vector.tensor_sub(xnst[:, b * T:(b + 1) * T], tmp32[:], Bmr[:]), 1)
        nc.sync.wait_ge(vs.h, vt)
        sts.inc(b, nc.sync.dma_start(fm_view(xn_d, c, 1)[:, 0, :], xnst[:, b * T:(b + 1) * T]))
    for b in range(2):
        sts.wait(nc.sync, b)



def emit_l0qkv(env, ctx, xn_d, wqkv_d, bqk_d, bvbc_d, q_d, k_d, vaug_d):
    nc = env.nc
    sb = lambda n, s, d=F32: ctx.enter_context(nc.sbuf_tensor(n, s, d))
    ld, wld, stm, vs, ss, pe = (env.sem("qk_" + x) for x in ["ld", "wld", "st", "v", "s", "pe"])

    xn_sb = sb("qk_xn", [128, NCH * T], BF16)
    wbuf = sb("qk_w", [128, 2 * NCH * 128], BF16)
    vw = sb("qk_vw", [128, NCH * 512], BF16)
    bqk = sb("qk_bqk", [128, 20])
    bvbc = sb("qk_bvbc", [128, 512])
    qs = sb("qk_qs", [128, 2 * T], BF16)
    vstg = sb("qk_vstg", [128, 2 * 520], BF16)
    pq = ctx.enter_context(nc.psum_tensor("qk_pq", [128, 2 * 512], F32))
    pv = ctx.enter_context(nc.psum_tensor("qk_pv", [128, 2 * 512], F32))

    ld.on(nc.sync.dma_start(xn_sb[:].rearrange("p (c t) -> p c t", c=NCH), fm_view(xn_d)), 16)
    ld.on(nc.sync.dma_start(bqk[:], bqk_d.ap()), 16)
    ld.on(nc.sync.dma_start(bvbc[:], bvbc_d.ap()), 16)
    PL = ld.on(nc.gpsimd.dma_start(vw[:].rearrange("p (c f) -> p c f", c=NCH),
                                   wqkv_d.ap().rearrange("(c p) f -> p c f", p=128)[:, :, 2560:3072]), 16)

    wlds = SlotSem(env, "qk_wlds", 2)
    wrot = Rot(env, "qk_wrot", 2)
    qrot = Rot(env, "qk_qrot", 2)
    prot = Rot(env, "qk_prot", 2)
    sts = SlotSem(env, "qk_sts", 2)
    vsts = SlotSem(env, "qk_vsts", 2)
    for fc in range(20):
        ws = wrot.acquire(nc.gpsimd)
        wb0 = ws * NCH * 128
        wlds.inc(ws, nc.gpsimd.dma_start(
            wbuf[:, wb0:wb0 + NCH * 128].rearrange("p (c f) -> p c f", c=NCH),
            wqkv_d.ap().rearrange("(c p) f -> p c f", p=128)[:, :, fc * 128:(fc + 1) * 128]))
        qslot = qrot.acquire(nc.scalar)
        for s in range(4):
            pslot = prot.acquire(nc.tensor)
            if fc == 0 and s == 0:
                nc.tensor.wait_ge(ld.h, PL)
            wlds.wait(nc.tensor, ws)
            for kc in range(NCH):
                i = nc.tensor.matmul(pq[:, pslot * 512:(pslot + 1) * 512],
                                     wbuf[:, wb0 + kc * 128: wb0 + (kc + 1) * 128],
                                     xn_sb[:, kc * T + s * 512: kc * T + (s + 1) * 512],
                                     start=(kc == 0), stop=(kc == NCH - 1))
            pt = pe.on(i, 1)
            if s == 3:
                wrot.done_at(pe, pt)
            nc.scalar.wait_ge(pe.h, pt)
            i = nc.scalar.activation(qs[:, qslot * T + s * 512: qslot * T + (s + 1) * 512],
                                     pq[:, pslot * 512:(pslot + 1) * 512], AF.Identity,
                                     bias=bqk[:, fc:fc + 1])
            et = ss.on(i, 1)
            prot.done_at(ss, et)
        nc.sync.wait_ge(ss.h, et)
        dst, c0 = (q_d, fc) if fc < 16 else (k_d, fc - 16)
        sts.inc(qslot, nc.sync.dma_start(fm_view(dst, c0, 1)[:, 0, :], qs[:, qslot * T:(qslot + 1) * T]))
        qrot.done_at(*sts.pair(qslot))

    vrot = Rot(env, "qk_vrot", 2)
    for tc in range(NTC):
        vslot = vrot.acquire(nc.vector)
        pslot = prot.acquire(nc.tensor)
        if tc == 0:
            nc.tensor.wait_ge(ld.h, PL)
            nc.vector.wait_ge(ld.h, PL)
        for kc in range(NCH):
            i = nc.tensor.matmul(pv[:, pslot * 512:(pslot + 1) * 512],
                                 xn_sb[:, kc * T + tc * 128: kc * T + (tc + 1) * 128],
                                 vw[:, kc * 512:(kc + 1) * 512],
                                 start=(kc == 0), stop=(kc == NCH - 1))
        pt = pe.on(i, 1)
        nc.vector.wait_ge(pe.h, pt)
        nc.vector.memset(vstg[:, vslot * 520:(vslot + 1) * 520], 1.0)
        i = nc.vector.tensor_add(
            vstg[:, vslot * 520:(vslot + 1) * 520].rearrange("p (g d) -> p g d", g=8)[:, :, 0:64],
            pv[:, pslot * 512:(pslot + 1) * 512].rearrange("p (g d) -> p g d", g=8),
            bvbc[:].rearrange("p (g d) -> p g d", g=8))
        vt = vs.on(i, 1)
        prot.done_at(vs, vt)
        nc.sync.wait_ge(vs.h, vt)
        vsts.inc(vslot, nc.sync.dma_start(vaug_d.ap()[tc * 128:(tc + 1) * 128, :, :],
                                          vstg[:, vslot * 520:(vslot + 1) * 520].rearrange("p (g d) -> p g d", g=8)))
        vrot.done_at(*vsts.pair(vslot))
    for sl_ in range(2):
        sts.wait(nc.sync, sl_)
        vsts.wait(nc.sync, sl_)



def emit_l0att(env, ctx, q_d, k_d, vaug_d, o_d):
    nc = env.nc
    sb = lambda n, s, d=F32: ctx.enter_context(nc.sbuf_tensor(n, s, d))
    ld, qld, stm, vs, ss, pe, ex = (env.sem("at_" + x) for x in
                                    ["ld", "qld", "st", "v", "s", "pe", "ex"])

    k_sb = sb("at_k", [64, 8 * T], BF16)
    vaug = sb("at_vaug", [128, NTC * 520], BF16)
    qh = sb("at_qh", [64, 2 * T], BF16)
    oh = sb("at_oh", [64, 2 * T], BF16)
    pb = sb("at_pb", [128, 4 * 512], BF16)
    rr = sb("at_rr", [1, 512])
    rhi = sb("at_rhi", [1, 512], BF16)
    rlo = sb("at_rlo", [1, 512], BF16)
    rb = sb("at_rb", [64, 512])
    onesbr = sb("at_ones", [1, 64], BF16)
    c_m20 = sb("at_m20", [128, 1])
    c_0125 = sb("at_0125", [128, 1])
    pS = ctx.enter_context(nc.psum_tensor("at_pS", [128, 4 * 512], F32))
    pPV = ctx.enter_context(nc.psum_tensor("at_pPV", [65, 2 * 512], F32))
    pB = ctx.enter_context(nc.psum_tensor("at_pB", [64, 512], F32))

    nc.gpsimd.memset(onesbr[:], 1.0)
    nc.gpsimd.memset(c_m20[:], -20.0)
    nc.gpsimd.memset(c_0125[:], 0.125)
    ld.on(nc.sync.dma_start(k_sb[:].rearrange("p (g t) -> p g t", g=8),
                            k_d.ap().rearrange("(g p) t -> p g t", p=64)), 16)
    PL = ld.on(nc.sync.dma_start(vaug[:].rearrange("p (c gd) -> p c gd", c=NTC),
                                 vaug_d.ap().rearrange("(c p) g d -> p c (g d)", p=128)), 16)

    qlds = SlotSem(env, "at_qlds", 2)
    osts = SlotSem(env, "at_osts", 2)
    qrot = Rot(env, "at_qrot", 2)
    orot = Rot(env, "at_orot", 2)
    brot = Rot(env, "at_brot", 4)
    srot = Rot(env, "at_srot", 4)
    pvrot = Rot(env, "at_pvrot", 2)
    for g in range(8):
        for r in range(4):
            h = g * 4 + r
            qslot = qrot.acquire(nc.sync)
            qlds.inc(qslot, nc.sync.dma_start(
                qh[:, qslot * T:(qslot + 1) * T],
                q_d.ap().rearrange("(c o p) t -> p (c o) t", p=64, o=2)[:, h, :]))
            oslot = orot.acquire(nc.vector)
            LAG = 3
            for s in range(4):
                pvslot = pvrot.acquire(nc.tensor)
                pvsl = pvslot * 512
                ets = {}
                bslots = {}
                for step in range(NTC + LAG):
                    if step < NTC:
                        kc = step
                        sslot = srot.acquire(nc.tensor)
                        if h == 0 and s == 0 and kc == 0:
                            nc.tensor.wait_ge(ld.h, PL)
                        if s == 0 and kc == 0:
                            qlds.wait(nc.tensor, qslot)
                        i = nc.tensor.matmul(pS[:, sslot * 512:(sslot + 1) * 512],
                                             k_sb[:, g * T + kc * 128: g * T + (kc + 1) * 128],
                                             qh[:, qslot * T + s * 512: qslot * T + (s + 1) * 512],
                                             start=True, stop=True)
                        pt = pe.on(i, 1)
                        bslot = brot.acquire(nc.scalar)
                        bslots[kc] = bslot
                        nc.scalar.wait_ge(pe.h, pt)
                        i = nc.scalar.activation(pb[:, bslot * 512:(bslot + 1) * 512],
                                                 pS[:, sslot * 512:(sslot + 1) * 512],
                                                 AF.Exp, bias=c_m20[:], scale=c_0125[:])
                        ets[kc] = ex.on(i, 1)
                        srot.done_at(ex, ets[kc])
                    if step >= LAG:
                        kc2 = step - LAG
                        nc.tensor.wait_ge(ex.h, ets[kc2])
                        i = nc.tensor.matmul(pPV[:, pvsl:pvsl + 512],
                                             vaug[:, kc2 * 520 + g * 65: kc2 * 520 + g * 65 + 65],
                                             pb[:, bslots[kc2] * 512:(bslots[kc2] + 1) * 512],
                                             start=(kc2 == 0), stop=(kc2 == NTC - 1))
                        brot.done(i)
                pvt = brot.sem.v
                nc.vector.wait_ge(brot.sem.h, pvt)
                nc.vector.reciprocal(rr[:], pPV[64:65, pvsl:pvsl + 512])
                vt = vs.on(nc.vector.tensor_copy(rhi[:], rr[:]), 1)
                nc.tensor.wait_ge(vs.h, vt)
                i = nc.tensor.matmul(pB[:], onesbr[:], rhi[:], start=True, stop=True)
                bt = pe.on(i, 1)
                nc.vector.wait_ge(pe.h, bt)
                nc.vector.tensor_copy(rb[:], pB[:])
                i = nc.vector.tensor_mul(oh[:, oslot * T + s * 512: oslot * T + (s + 1) * 512],
                                         pPV[0:64, pvsl:pvsl + 512], rb[:])
                vt = vs.on(i, 1)
                pvrot.done_at(vs, vt)
            nc.sync.wait_ge(vs.h, vt)
            osts.inc(oslot, nc.sync.dma_start(
                o_d.ap().rearrange("(c o p) t -> p (c o) t", p=64, o=2)[:, h, :],
                oh[:, oslot * T:(oslot + 1) * T]))
            orot.done_at(*osts.pair(oslot))
            qrot.done_at(*osts.pair(oslot))
    for sl_ in range(2):
        osts.wait(nc.sync, sl_)



def emit_proj_addres(env, ctx, tag, in_d, res_d, w_d, bias_d, out_d):
    nc = env.nc
    sb = lambda n, s, d=F32: ctx.enter_context(nc.sbuf_tensor(tag + n, s, d))
    ld, wld, stm, vs, pe = (env.sem(tag + x) for x in ["ld", "wld", "st", "v", "pe"])

    in_sb = sb("_in", [128, NCH * T], BF16)
    wbuf = sb("_w", [128, 2 * NCH * 128], BF16)
    bias = sb("_b", [128, 16])
    xres = sb("_xres", [128, 2 * 512])
    ostg = sb("_o", [128, 2 * T])
    po = ctx.enter_context(nc.psum_tensor(tag + "_po", [128, 2 * 512], F32))

    PL = ld.on(nc.sync.dma_start(in_sb[:].rearrange("p (c t) -> p c t", c=NCH), fm_view(in_d)), 16)
    if bias_d is not None:
        PL = ld.on(nc.sync.dma_start(bias[:], bias_d.ap()), 16)
    else:
        nc.gpsimd.memset(bias[:], 0.0)

    wlds = SlotSem(env, tag + "_wlds", 2)
    xlds = SlotSem(env, tag + "_xlds", 2)
    osts = SlotSem(env, tag + "_osts", 2)
    wrot = Rot(env, tag + "_wrot", 2)
    orot = Rot(env, tag + "_orot", 2)
    prot = Rot(env, tag + "_prot", 2)
    xrot = Rot(env, tag + "_xrot", 2)
    for fc in range(NCH):
        ws = wrot.acquire(nc.gpsimd)
        wb0 = ws * NCH * 128
        wlds.inc(ws, nc.gpsimd.dma_start(
            wbuf[:, wb0:wb0 + NCH * 128].rearrange("p (c f) -> p c f", c=NCH),
            w_d.ap().rearrange("(c p) f -> p c f", p=128)[:, :, fc * 128:(fc + 1) * 128]))
        oslot = orot.acquire(nc.vector)
        for s in range(4):
            xs = xrot.acquire(nc.sync)
            xlds.inc(xs, nc.sync.dma_start(xres[:, xs * 512:(xs + 1) * 512],
                                           fm_view(res_d, fc, 1, s * 512, 512)[:, 0, :]))
            pslot = prot.acquire(nc.tensor)
            if fc == 0 and s == 0:
                nc.tensor.wait_ge(ld.h, PL)
            wlds.wait(nc.tensor, ws)
            for kc in range(NCH):
                i = nc.tensor.matmul(po[:, pslot * 512:(pslot + 1) * 512],
                                     wbuf[:, wb0 + kc * 128: wb0 + (kc + 1) * 128],
                                     in_sb[:, kc * T + s * 512: kc * T + (s + 1) * 512],
                                     start=(kc == 0), stop=(kc == NCH - 1))
            pt = pe.on(i, 1)
            if s == 3:
                wrot.done_at(pe, pt)
            nc.vector.wait_ge(pe.h, pt)
            xlds.wait(nc.vector, xs)
            if fc == 0 and s == 0:
                nc.vector.wait_ge(ld.h, PL)
            i = nc.vector.scalar_tensor_tensor(ostg[:, oslot * T + s * 512: oslot * T + (s + 1) * 512],
                                               po[:, pslot * 512:(pslot + 1) * 512],
                                               bias[:, fc:fc + 1], xres[:, xs * 512:(xs + 1) * 512],
                                               ALU.add, ALU.add)
            vt = vs.on(i, 1)
            prot.done_at(vs, vt)
            xrot.done_at(vs, vt)
        nc.sync.wait_ge(vs.h, vt)
        osts.inc(oslot, nc.sync.dma_start(fm_view(out_d, fc, 1)[:, 0, :], ostg[:, oslot * T:(oslot + 1) * T]))
        orot.done_at(*osts.pair(oslot))
    for sl_ in range(2):
        osts.wait(nc.sync, sl_)


def emit_l0wo(env, ctx, o_d, x0_d, wout_d, bout_d, x1a_d):
    emit_proj_addres(env, ctx, "wo", o_d, x0_d, wout_d, bout_d, x1a_d)


def emit_l1wo(env, ctx, od_d, x1_d, wo_d, x2a_d):
    emit_proj_addres(env, ctx, "w1", od_d, x1_d, wo_d, None, x2a_d)



def emit_l0moe(env, ctx, xn_d, x1a_d, router_d, ebias_d, selmat_d, identf_d, gup_d, dwn_d, x1_d):
    nc = env.nc
    sb = lambda n, s, d=F32: ctx.enter_context(nc.sbuf_tensor(n, s, d))
    ld, wld, dld, stm, vs, ss, pe, gd = (env.sem("mo_" + x) for x in
                                         ["ld", "wld", "dld", "st", "v", "s", "pe", "gd"])
    HT = 1024
    xn_sb = sb("mo_xn", [128, NCH * HT], BF16)
    acc = sb("mo_acc", [128, NCH * HT])
    rout = sb("mo_rt", [128, NCH * 16], BF16)
    ebias = sb("mo_eb", [128, 16])
    selm = sb("mo_sel", [16, 16 * 128])
    identf = sb("mo_idf", [128, 128])
    comb = sb("mo_comb", [128, 8 * 16])
    zwork = sb("mo_z", [128, 16])
    mcol = sb("mo_m", [128, 4])
    maskw = sb("mo_mask", [128, 16])
    ssum = sb("mo_ssum", [128, 2])
    cfm = sb("mo_cfm", [16, HT])
    ce = sb("mo_ce", [128, HT])
    hstash = sb("mo_h", [128, 6 * HT], BF16)
    gw = sb("mo_gw", [128, 2 * NCH * 256], BF16)
    dwb = sb("mo_dw", [128, 2 * 6 * 2048], BF16)
    gsil = sb("mo_gsil", [128, 2 * 512])
    ucs = sb("mo_uc", [128, 2 * 512])
    pg = ctx.enter_context(nc.psum_tensor("mo_pg", [128, 2 * 512], F32))
    pu = ctx.enter_context(nc.psum_tensor("mo_pu", [128, 2 * 512], F32))
    pd = ctx.enter_context(nc.psum_tensor("mo_pd", [128, 2 * 512], F32))
    aux = ctx.enter_context(nc.psum_tensor("mo_aux", [128, 512], F32))
    pr = aux[:, 0:16]
    pt_ = aux[0:16, 128:256]
    pce = aux[:, 0:512]

    ld.on(nc.gpsimd.dma_start(rout[:].rearrange("p (c e) -> p c e", c=NCH),
                              router_d.ap().rearrange("(c p) e -> p c e", p=128)), 16)
    ld.on(nc.sync.dma_start(ebias[:], ebias_d.ap()), 16)
    ld.on(nc.sync.dma_start(selm[:], selmat_d.ap()), 16)
    PL = ld.on(nc.sync.dma_start(identf[:], identf_d.ap()), 16)

    glds = SlotSem(env, "mo_glds", 2)
    dlds = SlotSem(env, "mo_dlds", 2)
    hlds = SlotSem(env, "mo_hlds", 2)
    grot = Rot(env, "mo_grot", 2)
    pgrot = Rot(env, "mo_pgrot", 2)
    pdrot = Rot(env, "mo_pdrot", 2)
    groups = [(0, 1), (2, 3), (4, 5), (6, 7), (8, 9), (10, 11), (12, 13), (14, 15), (16,)]
    ngrp_done = 0

    for half in range(2):
        t0 = half * HT
        if half == 1:
            nc.sync.wait_ge(stm.h, stm.v)
            nc.sync.wait_ge(pe.h, pe.v)
        hlds.inc(0, nc.sync.dma_start(xn_sb[:].rearrange("p (c t) -> p c t", c=NCH),
                                      fm_view(xn_d, 0, NCH, t0, HT)))
        hlds.inc(1, nc.sync.dma_start(acc[:].rearrange("p (c t) -> p c t", c=NCH),
                                      fm_view(x1a_d, 0, NCH, t0, HT)))
        for tc in range(8):
            if half == 0 and tc == 0:
                nc.tensor.wait_ge(ld.h, PL)
            hlds.wait(nc.tensor, 0)
            if tc > 0 or half == 1:
                nc.tensor.wait_ge(vs.h, vs.v)
            for kc in range(NCH):
                i = nc.tensor.matmul(pr, xn_sb[:, kc * HT + tc * 128: kc * HT + (tc + 1) * 128],
                                     rout[:, kc * 16:(kc + 1) * 16],
                                     start=(kc == 0), stop=(kc == NCH - 1))
            ptv = pe.on(i, 1)
            nc.scalar.wait_ge(pe.h, ptv)
            stq = ss.on(nc.scalar.activation(comb[:, tc * 16:(tc + 1) * 16], pr, AF.Sigmoid), 1)
            nc.vector.wait_ge(ss.h, stq)
            def _w(instr):
                t = vs.on(instr, 1)
                nc.vector.wait_ge(vs.h, t)
            _w(nc.vector.tensor_add(zwork[:], comb[:, tc * 16:(tc + 1) * 16], ebias[:]))
            for j in range(4):
                _w(nc.vector.reduce_max(mcol[:, j:j + 1], zwork[:], AX.X))
                if j < 3:
                    _w(nc.vector.tensor_scalar(maskw[:], zwork[:], mcol[:, j:j + 1], None, ALU.is_equal))
                    _w(nc.vector.scalar_tensor_tensor(zwork[:], maskw[:], -1e30, zwork[:], ALU.mult, ALU.add))
            _w(nc.vector.tensor_add(zwork[:], comb[:, tc * 16:(tc + 1) * 16], ebias[:]))
            _w(nc.vector.tensor_scalar(maskw[:], zwork[:], mcol[:, 3:4], None, ALU.is_ge))
            _w(nc.vector.tensor_mul(comb[:, tc * 16:(tc + 1) * 16], comb[:, tc * 16:(tc + 1) * 16], maskw[:]))
            _w(nc.vector.reduce_sum(ssum[:, 0:1], comb[:, tc * 16:(tc + 1) * 16], AX.X))
            _w(nc.vector.tensor_scalar_add(ssum[:, 0:1], ssum[:, 0:1], 1e-20))
            _w(nc.vector.reciprocal(ssum[:, 1:2], ssum[:, 0:1]))
            vt = vs.on(nc.vector.tensor_scalar(comb[:, tc * 16:(tc + 1) * 16],
                                               comb[:, tc * 16:(tc + 1) * 16],
                                               ssum[:, 1:2], None, ALU.mult), 1)
        for tc in range(8):
            if tc == 0:
                nc.tensor.wait_ge(vs.h, vs.v)
            else:
                nc.tensor.wait_ge(vs.h, vt)
            i = nc.tensor.transpose(pt_, comb[:, tc * 16:(tc + 1) * 16], identf[:])
            ptv = pe.on(i, 1)
            nc.vector.wait_ge(pe.h, ptv)
            vt = vs.on(nc.vector.tensor_copy(cfm[:, tc * 128:(tc + 1) * 128], pt_), 1)

        if half == 0:
            env._dwb_last = {}
        for e in range(17):
            eg = half * 17 + e
            par = e % 2
            if e < 16:
                for s2 in range(2):
                    if e == 0 and s2 == 0:
                        nc.tensor.wait_ge(vs.h, vs.v)
                    if s2 == 0:
                        nc.tensor.wait_ge(gd.h, max(0, eg - 1))
                    i = nc.tensor.matmul(pce, selm[:, e * 128:(e + 1) * 128],
                                         cfm[:, s2 * 512:(s2 + 1) * 512], start=True, stop=True)
                    ptv = pe.on(i, 1)
                    nc.vector.wait_ge(pe.h, ptv)
                    vt = vs.on(nc.vector.tensor_copy(ce[:, s2 * 512:(s2 + 1) * 512], pce), 1)
                    nc.tensor.wait_ge(vs.h, vt)
            prev = env._dwb_last.get(par)
            if prev is not None:
                nc.gpsimd.wait_ge(gd.h, prev + 1)
            env._dwb_last[par] = eg
            dlds.inc(par, nc.gpsimd.dma_start(
                dwb[:, par * 6 * 2048:(par + 1) * 6 * 2048].rearrange("p (c f) -> p c f", c=6),
                dwn_d.ap()[e].rearrange("(c p) f -> p c f", p=128)))
            for ic in range(6):
                gs_ = grot.acquire(nc.gpsimd)
                base = gs_ * NCH * 256
                glds.inc(gs_, nc.gpsimd.dma_start(
                    gw[:, base:base + NCH * 256].rearrange("p (c f) -> p c f", c=NCH)[:, :, 0:128],
                    gup_d.ap()[e].rearrange("(c p) f -> p c f", p=128)[:, :, ic * 128:(ic + 1) * 128]))
                glds.inc(gs_, nc.gpsimd.dma_start(
                    gw[:, base:base + NCH * 256].rearrange("p (c f) -> p c f", c=NCH)[:, :, 128:256],
                    gup_d.ap()[e].rearrange("(c p) f -> p c f", p=128)[:, :, 768 + ic * 128: 768 + (ic + 1) * 128]))
                for s2 in range(2):
                    pgs = pgrot.acquire(nc.tensor)
                    glds.wait(nc.tensor, gs_)
                    for kc in range(NCH):
                        nc.tensor.matmul(pg[:, pgs * 512:(pgs + 1) * 512],
                                         gw[:, base + kc * 256: base + kc * 256 + 128],
                                         xn_sb[:, kc * HT + s2 * 512: kc * HT + (s2 + 1) * 512],
                                         start=(kc == 0), stop=(kc == NCH - 1))
                    for kc in range(NCH):
                        i = nc.tensor.matmul(pu[:, pgs * 512:(pgs + 1) * 512],
                                             gw[:, base + kc * 256 + 128: base + kc * 256 + 256],
                                             xn_sb[:, kc * HT + s2 * 512: kc * HT + (s2 + 1) * 512],
                                             start=(kc == 0), stop=(kc == NCH - 1))
                    ptv = pe.on(i, 1)
                    if s2 == 1:
                        grot.done_at(pe, ptv)
                    nc.scalar.wait_ge(pe.h, ptv)
                    stq = ss.on(nc.scalar.activation(gsil[:, pgs * 512:(pgs + 1) * 512],
                                                     pg[:, pgs * 512:(pgs + 1) * 512], AF.Silu), 1)
                    nc.vector.wait_ge(ss.h, stq)
                    if ic == 0 and s2 == 0 and eg > 0:
                        nc.vector.wait_ge(gd.h, eg)
                    if e < 16:
                        nc.vector.tensor_mul(ucs[:, pgs * 512:(pgs + 1) * 512],
                                             pu[:, pgs * 512:(pgs + 1) * 512],
                                             ce[:, s2 * 512:(s2 + 1) * 512])
                    else:
                        nc.vector.tensor_copy(ucs[:, pgs * 512:(pgs + 1) * 512],
                                              pu[:, pgs * 512:(pgs + 1) * 512])
                    i = nc.vector.tensor_mul(
                        hstash[:, ic * HT + s2 * 512: ic * HT + (s2 + 1) * 512],
                        gsil[:, pgs * 512:(pgs + 1) * 512], ucs[:, pgs * 512:(pgs + 1) * 512])
                    hv = vs.on(i, 1)
                    pgrot.done_at(vs, hv)
            nc.tensor.wait_ge(vs.h, vs.v)
            dlds.wait(nc.tensor, par)
            for s2 in range(2):
                for fc in range(NCH):
                    pds = pdrot.acquire(nc.tensor)
                    for ic in range(6):
                        i = nc.tensor.matmul(
                            pd[:, pds * 512:(pds + 1) * 512],
                            dwb[:, par * 6 * 2048 + ic * 2048 + fc * 128: par * 6 * 2048 + ic * 2048 + (fc + 1) * 128],
                            hstash[:, ic * HT + s2 * 512: ic * HT + (s2 + 1) * 512],
                            start=(ic == 0), stop=(ic == 5))
                    ptv = pe.on(i, 1)
                    if s2 == 1 and fc == NCH - 1:
                        gd.on(nc.tensor.nop(), 1)
                    nc.vector.wait_ge(pe.h, ptv)
                    if e == 0 and s2 == 0 and fc == 0:
                        hlds.wait(nc.vector, 1)
                    i = nc.vector.tensor_add(acc[:, fc * HT + s2 * 512: fc * HT + (s2 + 1) * 512],
                                             acc[:, fc * HT + s2 * 512: fc * HT + (s2 + 1) * 512],
                                             pd[:, pds * 512:(pds + 1) * 512])
                    av = vs.on(i, 1)
                    pdrot.done_at(vs, av)
        nc.sync.wait_ge(vs.h, vs.v)
        stm.on(nc.sync.dma_start(fm_view(x1_d, 0, NCH, t0, HT),
                                 acc[:].rearrange("p (c t) -> p c t", c=NCH)), 16)
    nc.sync.wait_ge(stm.h, stm.v)



def emit_l1proj(env, ctx, xn_d, wq_d, wk_d, wv_d, wb_d, cwq_d, cwk_d, cwv_d,
                qn_d, kn_d, v_d, beta_d):
    nc = env.nc
    sb = lambda n, s, d=F32: ctx.enter_context(nc.sbuf_tensor(n, s, d))
    ld, wld, stm, vs, ss, pe = (env.sem("p1_" + x) for x in ["ld", "wld", "st", "v", "s", "pe"])

    xn_sb = sb("p1_xn", [128, NCH * T], BF16)
    wbuf = sb("p1_w", [128, 2 * NCH * 128], BF16)
    wbsb = sb("p1_wb", [128, NCH * 16], BF16)
    cw = sb("p1_cw", [128, 3 * 64])
    raw = sb("p1_raw", [128, 2 * (T + 3)])
    conv = sb("p1_conv", [128, T])
    xbf = sb("p1_xbf", [128, T], BF16)
    sqb = sb("p1_sq", [128, T], BF16)
    lnrow = sb("p1_lnr", [1, 512])
    rstd_bf = sb("p1_rbf", [1, 512], BF16)
    onesb = sb("p1_onesb", [128, 1], BF16)
    onesrb = sb("p1_onesrb", [1, 128], BF16)
    outst = sb("p1_out", [128, 2 * T], BF16)
    btst = sb("p1_bt", [128, 2 * 16])
    eps_t = sb("p1_eps", [1, 1])
    mh = sb("p1_mh", [1, 1])
    lnb_q = sb("p1_lnbq", [1, 1])
    pp = ctx.enter_context(nc.psum_tensor("p1_pp", [128, 2 * 512], F32))
    pn = ctx.enter_context(nc.psum_tensor("p1_pn", [2, 512], F32))
    pbc = ctx.enter_context(nc.psum_tensor("p1_pbc", [128, 512], F32))
    pbeta0 = ctx.enter_context(nc.psum_tensor("p1_pbeta0", [128, 16], F32))
    pbeta1 = ctx.enter_context(nc.psum_tensor("p1_pbeta1", [128, 16], F32))
    pbetas = [pbeta0, pbeta1]

    nc.gpsimd.memset(onesb[:], 1.0)
    nc.gpsimd.memset(onesrb[:], 1.0)
    nc.gpsimd.memset(eps_t[:], 1e-6)
    nc.gpsimd.memset(mh[:], -0.5)
    nc.gpsimd.memset(lnb_q[:], float(-0.5 * np.log(128.0)))
    ld.on(nc.sync.dma_start(xn_sb[:].rearrange("p (c t) -> p c t", c=NCH), fm_view(xn_d)), 16)
    ld.on(nc.sync.dma_start(cw[:, 0:64], cwq_d.ap()), 16)
    ld.on(nc.sync.dma_start(cw[:, 64:128], cwk_d.ap()), 16)
    ld.on(nc.sync.dma_start(cw[:, 128:192], cwv_d.ap()), 16)
    PL = ld.on(nc.gpsimd.dma_start(wbsb[:].rearrange("p (c e) -> p c e", c=NCH),
                                   wb_d.ap().rearrange("(c p) e -> p c e", p=128)), 16)
    wlds = SlotSem(env, "p1_wlds", 2)
    bsts = SlotSem(env, "p1_bsts", 2)
    osts = SlotSem(env, "p1_osts", 2)

    import os as _os
    brot = Rot(env, "p1_brot", 2)
    for tc in (range(NTC) if _os.environ.get("L1PROJ_BETA", "1") == "1" else []):
        bs = brot.acquire(nc.tensor, nc.scalar)
        if tc == 0:
            nc.tensor.wait_ge(ld.h, PL)
        for kc in range(NCH):
            i = nc.tensor.matmul(pbetas[bs][:],
                                 xn_sb[:, kc * T + tc * 128: kc * T + (tc + 1) * 128],
                                 wbsb[:, kc * 16:(kc + 1) * 16],
                                 start=(kc == 0), stop=(kc == NCH - 1))
        ptv = pe.on(i, 1)
        nc.scalar.wait_ge(pe.h, ptv)
        i = nc.scalar.activation(btst[:, bs * 16:(bs + 1) * 16], pbetas[bs][:], AF.Sigmoid)
        stq = ss.on(i, 1)
        nc.sync.wait_ge(ss.h, stq)
        bsts.inc(bs, nc.sync.dma_start(beta_d.ap()[tc * 128:(tc + 1) * 128, :], btst[:, bs * 16:(bs + 1) * 16]))
        brot.done_at(*bsts.pair(bs))

    wrot = Rot(env, "p1_wrot", 2)
    rrot = Rot(env, "p1_rrot", 2)
    prot = Rot(env, "p1_prot", 2)
    orot = Rot(env, "p1_orot", 2)
    import os
    _tis = [int(x) for x in os.environ.get("L1PROJ_TI", "012")]
    for ti, (w_d, cwoff, out_d, ln) in enumerate([(wq_d, 0, qn_d, True),
                                                  (wk_d, 64, kn_d, True),
                                                  (wv_d, 128, v_d, False)]):
        if ti not in _tis:
            continue
        for fc in range(NCH):
            ws = wrot.acquire(nc.gpsimd)
            wb0 = ws * NCH * 128
            wlds.inc(ws, nc.gpsimd.dma_start(
                wbuf[:, wb0:wb0 + NCH * 128].rearrange("p (c f) -> p c f", c=NCH),
                w_d.ap().rearrange("(c p) f -> p c f", p=128)[:, :, fc * 128:(fc + 1) * 128]))
            rs = rrot.acquire(nc.vector)
            rbase = rs * (T + 3)
            for s in range(4):
                pslot = prot.acquire(nc.tensor)
                wlds.wait(nc.tensor, ws)
                for kc in range(NCH):
                    i = nc.tensor.matmul(pp[:, pslot * 512:(pslot + 1) * 512],
                                         wbuf[:, wb0 + kc * 128: wb0 + (kc + 1) * 128],
                                         xn_sb[:, kc * T + s * 512: kc * T + (s + 1) * 512],
                                         start=(kc == 0), stop=(kc == NCH - 1))
                ptv = pe.on(i, 1)
                if s == 3:
                    wrot.done_at(pe, ptv)
                nc.vector.wait_ge(pe.h, ptv)
                if s == 0:
                    nc.vector.memset(raw[:, rbase:rbase + 3], 0.0)
                    if ti == 0 and fc == 0:
                        nc.vector.wait_ge(ld.h, PL)
                i = nc.vector.tensor_copy(raw[:, rbase + 3 + s * 512: rbase + 3 + (s + 1) * 512],
                                          pp[:, pslot * 512:(pslot + 1) * 512])
                cv = vs.on(i, 1)
                prot.done_at(vs, cv)
            nc.vector.tensor_scalar(conv[:], raw[:, rbase + 0: rbase + T],
                                    cw[:, cwoff + fc * 4 + 0: cwoff + fc * 4 + 1], None, ALU.mult)
            for tap in range(1, 4):
                i = nc.vector.scalar_tensor_tensor(conv[:], raw[:, rbase + tap: rbase + tap + T],
                                                   cw[:, cwoff + fc * 4 + tap: cwoff + fc * 4 + tap + 1],
                                                   conv[:], ALU.mult, ALU.add)
            vt = vs.on(i, 1)
            rrot.done_at(vs, vt)
            oslot = orot.acquire(nc.scalar)
            nc.scalar.wait_ge(vs.h, vt)
            if not ln:
                stq = ss.on(nc.scalar.activation(outst[:, oslot * T:(oslot + 1) * T], conv[:], AF.Silu), 1)
                nc.sync.wait_ge(ss.h, stq)
            else:
                ss.on(nc.scalar.activation(conv[:], conv[:], AF.Silu), 1)
                stq = ss.on(nc.scalar.activation(xbf[:], conv[:], AF.Copy), 1)
                nc.scalar.wait_ge(ss.h, stq)
                stq = ss.on(nc.scalar.activation(sqb[:], xbf[:], AF.Square), 1)
                for s in range(4):
                    nc.tensor.wait_ge(ss.h, stq)
                    if s > 0 or True:
                        nc.tensor.wait_ge(vs.h, vs.v)
                    i = nc.tensor.matmul(pn[0:1, :], onesb[:], sqb[:, s * 512:(s + 1) * 512],
                                         start=True, stop=True)
                    ptv = pe.on(i, 1)
                    nc.scalar.wait_ge(pe.h, ptv)
                    nc.scalar.activation(lnrow[:], pn[0:1, :], AF.Ln,
                                         bias=eps_t[:])
                    if ti == 0:
                        i = nc.scalar.activation(lnrow[:], lnrow[:], AF.Exp, scale=mh[:], bias=lnb_q[:])
                    else:
                        i = nc.scalar.activation(lnrow[:], lnrow[:], AF.Exp, scale=mh[:])
                    st2 = ss.on(i, 1)
                    nc.vector.wait_ge(ss.h, st2)
                    vt2 = vs.on(nc.vector.tensor_copy(rstd_bf[:], lnrow[:]), 1)
                    nc.tensor.wait_ge(vs.h, vt2)
                    i = nc.tensor.matmul(pbc[:], onesrb[:], rstd_bf[:], start=True, stop=True)
                    ptv = pe.on(i, 1)
                    nc.vector.wait_ge(pe.h, ptv)
                    i = nc.vector.tensor_mul(outst[:, oslot * T + s * 512: oslot * T + (s + 1) * 512],
                                             conv[:, s * 512:(s + 1) * 512], pbc[:])
                    vt = vs.on(i, 1)
                nc.sync.wait_ge(vs.h, vt)
            osts.inc(oslot, nc.sync.dma_start(fm_view(out_d, fc, 1)[:, 0, :],
                                              outst[:, oslot * T:(oslot + 1) * T]))
            orot.done_at(*osts.pair(oslot))
    for sl_ in range(2):
        osts.wait(nc.sync, sl_)
        bsts.wait(nc.sync, sl_)



def emit_l1gate(env, ctx, xn_d, wg_d, normw_d, gate_d):
    nc = env.nc
    sb = lambda n, s, d=F32: ctx.enter_context(nc.sbuf_tensor(n, s, d))
    ld, wld, stm, vs, ss, pe = (env.sem("g1_" + x) for x in ["ld", "wld", "st", "v", "s", "pe"])

    xn_sb = sb("g1_xn", [128, NCH * T], BF16)
    wbuf = sb("g1_w", [128, 2 * NCH * 128], BF16)
    nw = sb("g1_nw", [128, 1])
    sg = sb("g1_sg", [128, 512])
    ostg = sb("g1_o", [128, 2 * T], BF16)
    pp = ctx.enter_context(nc.psum_tensor("g1_pp", [128, 2 * 512], F32))

    ld.on(nc.sync.dma_start(xn_sb[:].rearrange("p (c t) -> p c t", c=NCH), fm_view(xn_d)), 16)
    PL = ld.on(nc.sync.dma_start(nw[:], normw_d.ap()), 16)

    wlds = SlotSem(env, "g1_wlds", 2)
    osts = SlotSem(env, "g1_osts", 2)
    wrot = Rot(env, "g1_wrot", 2)
    orot = Rot(env, "g1_orot", 2)
    prot = Rot(env, "g1_prot", 2)
    for fc in range(NCH):
        ws = wrot.acquire(nc.gpsimd)
        wb0 = ws * NCH * 128
        wlds.inc(ws, nc.gpsimd.dma_start(
            wbuf[:, wb0:wb0 + NCH * 128].rearrange("p (c f) -> p c f", c=NCH),
            wg_d.ap().rearrange("(c p) f -> p c f", p=128)[:, :, fc * 128:(fc + 1) * 128]))
        oslot = orot.acquire(nc.scalar)
        for s in range(4):
            pslot = prot.acquire(nc.tensor)
            if fc == 0 and s == 0:
                nc.tensor.wait_ge(ld.h, PL)
            wlds.wait(nc.tensor, ws)
            for kc in range(NCH):
                i = nc.tensor.matmul(pp[:, pslot * 512:(pslot + 1) * 512],
                                     wbuf[:, wb0 + kc * 128: wb0 + (kc + 1) * 128],
                                     xn_sb[:, kc * T + s * 512: kc * T + (s + 1) * 512],
                                     start=(kc == 0), stop=(kc == NCH - 1))
            ptv = pe.on(i, 1)
            if s == 3:
                wrot.done_at(pe, ptv)
            nc.scalar.wait_ge(pe.h, ptv)
            stq = ss.on(nc.scalar.activation(sg[:], pp[:, pslot * 512:(pslot + 1) * 512], AF.Sigmoid), 1)
            nc.vector.wait_ge(ss.h, stq)
            if fc == 0 and s == 0:
                nc.vector.wait_ge(ld.h, PL)
            i = nc.vector.tensor_scalar(ostg[:, oslot * T + s * 512: oslot * T + (s + 1) * 512],
                                        sg[:], nw[:, 0:1], None, ALU.mult)
            vt = vs.on(i, 1)
            prot.done_at(vs, vt)
        nc.sync.wait_ge(vs.h, vt)
        osts.inc(oslot, nc.sync.dma_start(fm_view(gate_d, fc, 1)[:, 0, :], ostg[:, oslot * T:(oslot + 1) * T]))
        orot.done_at(*osts.pair(oslot))
    for sl_ in range(2):
        osts.wait(nc.sync, sl_)



def emit_l1delta(env, ctx, qn_d, kn_d, v_d, beta_d, gate_d, masksl_d, maskui_d,
                 maskbd_d, maskbdT_d, maskblnT_d, identbf_d, od_d):
    nc = env.nc
    NH = 16
    NLEV = 4
    NS = 4
    sb = lambda n, s, d=F32: ctx.enter_context(nc.sbuf_tensor(n, s, d))
    ld, stm, vs, ss, pe = (env.sem("d1_" + x) for x in ["ld", "st", "v", "s", "pe"])

    kc_sb = sb("d1_k", [128, 2 * NH * 128], BF16)
    qc_sb = sb("d1_q", [128, 2 * NH * 128], BF16)
    vc_sb = sb("d1_vv", [128, 2 * NH * 128], BF16)
    gc_sb = sb("d1_g", [128, 2 * NH * 128], BF16)
    bt_sb = sb("d1_b", [128, 2 * 16])
    S_sb = sb("d1_S", [128, NH * 128])
    Sbf = sb("d1_Sbf", [128, NH * 128], BF16)
    masksl = sb("d1_msl", [128, 128])
    maskui = sb("d1_mui", [128, 128])
    maskbd = sb("d1_mbd", [128, 128])
    maskbdT = sb("d1_mbdT", [128, 128])
    maskblnT = sb("d1_mblnT", [128, 128])
    identb = sb("d1_idb", [128, 128], BF16)
    identf32 = sb("d1_idf32", [128, 128])
    odst = sb("d1_od", [128, 2 * NH * 128], BF16)
    epsc = sb("d1_epsc", [128, 1])
    mhc = sb("d1_mh", [128, 1])
    i128 = sb("d1_i128", [128, 1])
    scr = []
    for j in range(NS):
        scr.append(dict(
            ch=[sb(f"d1_chA{j}", [128, 128], BF16), sb(f"d1_chB{j}", [128, 128], BF16)],
            chp=[sb(f"d1_chAp{j}", [128, 128], BF16), sb(f"d1_chBp{j}", [128, 128], BF16)],
            Pp=sb(f"d1_P{j}", [128, 128], BF16),
            vtm=sb(f"d1_vtm{j}", [128, 128], BF16),
            ktm=sb(f"d1_ktm{j}", [128, 128], BF16),
            Wsb=sb(f"d1_W{j}", [128, 128], BF16),
            Apr=sb(f"d1_A{j}", [128, 128], BF16),
            rhs32=sb(f"d1_rhs{j}", [128, 128]),
            sq32=sb(f"d1_sq{j}", [128, 128]),
            rstd=sb(f"d1_rstd{j}", [128, 4]),
            onbf=sb(f"d1_onbf{j}", [128, 128], BF16),
            Mpr=sb(f"d1_Mpr{j}", [128, 128], BF16),
            lnegT=sb(f"d1_lnegT{j}", [128, 128], BF16),
            zbf=sb(f"d1_zbf{j}", [128, 128], BF16),
            ybf=sb(f"d1_ybf{j}", [128, 128], BF16),
            Wacc=sb(f"d1_Wacc{j}", [128, 128]),
        ))

    pA = ctx.enter_context(nc.psum_tensor("d1_pA", [128, 2 * 512], F32))
    pO = ctx.enter_context(nc.psum_tensor("d1_pO", [128, 2 * 512], F32))
    pT0 = ctx.enter_context(nc.psum_tensor("d1_pT0", [128, 128], BF16))
    pT1 = ctx.enter_context(nc.psum_tensor("d1_pT1", [128, 128], BF16))

    nc.gpsimd.memset(epsc[:], 1e-5)
    nc.gpsimd.memset(mhc[:], -0.5)
    nc.gpsimd.memset(i128[:], 1.0 / 128.0)
    nc.gpsimd.memset(S_sb[:], 0.0)
    nc.gpsimd.memset(Sbf[:], 0.0)
    ld.on(nc.sync.dma_start(masksl[:], masksl_d.ap()), 16)
    ld.on(nc.sync.dma_start(maskui[:], maskui_d.ap()), 16)
    ld.on(nc.sync.dma_start(maskbd[:], maskbd_d.ap()), 16)
    ld.on(nc.sync.dma_start(maskbdT[:], maskbdT_d.ap()), 16)
    ld.on(nc.sync.dma_start(maskblnT[:], maskblnT_d.ap()), 16)
    PL = ld.on(nc.sync.dma_start(identb[:], identbf_d.ap()), 16)
    clds = SlotSem(env, "d1_clds", 2)
    odss = SlotSem(env, "d1_odss", 2)
    nc.vector.wait_ge(ld.h, PL)
    vt0 = vs.on(nc.vector.tensor_copy(identf32[:], identb[:]), 1)

    crot = Rot(env, "d1_crot", 2)
    orot = Rot(env, "d1_orot", 2)
    parot = Rot(env, "d1_parot", 2)
    porot = Rot(env, "d1_porot", 2)
    ptrot = [Rot(env, "d1_pt0", 1), Rot(env, "d1_pt1", 1)]

    def hsteps(c, cs, os_, h, S):
        cb = cs * NH * 128
        ob = os_ * NH * 128
        kq = cb + h * 128
        khs = kc_sb[:, kq:kq + 128]
        qhs = qc_sb[:, kq:kq + 128]
        vhs = vc_sb[:, kq:kq + 128]
        ghs = gc_sb[:, kq:kq + 128]
        bcol = bt_sb[:, cs * 16 + h: cs * 16 + h + 1]
        Ssl = S_sb[:, h * 128:(h + 1) * 128]
        Sbsl = Sbf[:, h * 128:(h + 1) * 128]
        ch, chp, Pp = S["ch"], S["chp"], S["Pp"]

        pa = parot.acquire(nc.tensor)
        if h == 0:
            clds.wait(nc.tensor, cs)
            if c == 0:
                nc.tensor.wait_ge(vs.h, vt0)
        i = nc.tensor.matmul(pA[:, pa * 512: pa * 512 + 128], khs, khs, start=True, stop=True)
        t1 = pe.on(i, 1)
        nc.vector.wait_ge(pe.h, t1)
        if h == 0:
            clds.wait(nc.vector, cs)
        i = nc.vector.scalar_tensor_tensor(S["Mpr"][:], pA[:, pa * 512: pa * 512 + 128], bcol,
                                           masksl[:], ALU.mult, ALU.mult)
        tM = vs.on(i, 1)
        parot.done_at(vs, tM)
        i = nc.vector.tensor_mul(ch[0][:], S["Mpr"][:], maskbd[:])
        vs.on(i, 1)
        pts = ptrot[0].acquire(nc.tensor)
        nc.tensor.wait_ge(vs.h, tM)
        i = nc.tensor.transpose(pT0[:], S["Mpr"][:], identb[:])
        t2 = pe.on(i, 1)
        nc.scalar.wait_ge(pe.h, t2)
        i = nc.scalar.activation(chp[0][:], pT0[:], AF.Copy)
        tMp = ss.on(i, 1)
        ptrot[0].done_at(ss, tMp)
        nc.vector.wait_ge(ss.h, tMp)
        nc.vector.tensor_mul(S["lnegT"][:], chp[0][:], maskblnT[:])
        i = nc.vector.tensor_mul(chp[0][:], chp[0][:], maskbdT[:])
        tMp = vs.on(i, 1)
        tP = vs.on(nc.vector.tensor_sub(Pp[:], identf32[:], chp[0][:]), 1)
        yield
        cur = 0
        prev_chp_t = None
        for lev in range(1, NLEV + 1):
            nxt = 1 - cur
            pa = parot.acquire(nc.tensor)
            if lev == 1:
                nc.tensor.wait_ge(vs.h, tP)
            else:
                nc.tensor.wait_ge(ss.h, prev_chp_t)
            i = nc.tensor.matmul(pA[:, pa * 512: pa * 512 + 128], chp[cur][:], ch[cur][:],
                                 start=True, stop=True)
            t3 = pe.on(i, 1)
            nc.scalar.wait_ge(pe.h, t3)
            i = nc.scalar.activation(ch[nxt][:], pA[:, pa * 512: pa * 512 + 128], AF.Copy)
            t5 = ss.on(i, 1)
            parot.done_at(ss, t5)
            if lev < NLEV:
                pa = parot.acquire(nc.tensor)
                i = nc.tensor.matmul(pA[:, pa * 512: pa * 512 + 128], ch[cur][:], chp[cur][:],
                                     start=True, stop=True)
                t4 = pe.on(i, 1)
                nc.scalar.wait_ge(pe.h, t4)
                i = nc.scalar.activation(chp[nxt][:], pA[:, pa * 512: pa * 512 + 128], AF.Copy)
                t5b = ss.on(i, 1)
                parot.done_at(ss, t5b)
                prev_chp_t = t5b
            po_ = porot.acquire(nc.tensor)
            nc.tensor.wait_ge(ss.h, t5)
            nc.tensor.wait_ge(vs.h, tP)
            i = nc.tensor.matmul(pO[:, po_ * 512: po_ * 512 + 128], ch[nxt][:], Pp[:],
                                 start=True, stop=True)
            t7 = pe.on(i, 1)
            nc.vector.wait_ge(pe.h, t7)
            i = nc.vector.tensor_add(Pp[:], Pp[:], pO[:, po_ * 512: po_ * 512 + 128])
            tP = vs.on(i, 1)
            porot.done_at(vs, tP)
            cur = nxt
            yield
        pts = ptrot[0].acquire(nc.tensor)
        i = nc.tensor.transpose(pT0[:], vhs, identb[:])
        t8 = pe.on(i, 1)
        nc.scalar.wait_ge(pe.h, t8)
        i = nc.scalar.activation(S["vtm"][:], pT0[:], AF.Copy)
        t9 = ss.on(i, 1)
        ptrot[0].done_at(ss, t9)
        pts = ptrot[1].acquire(nc.tensor)
        i = nc.tensor.transpose(pT1[:], khs, identb[:])
        t10 = pe.on(i, 1)
        nc.scalar.wait_ge(pe.h, t10)
        i = nc.scalar.activation(S["ktm"][:], pT1[:], AF.Copy)
        t11 = ss.on(i, 1)
        ptrot[1].done_at(ss, t11)
        po_ = porot.acquire(nc.tensor)
        i = nc.tensor.matmul(pO[:, po_ * 512: po_ * 512 + 128], khs, Sbsl, start=True, stop=True)
        t12 = pe.on(i, 1)
        nc.vector.wait_ge(pe.h, t12)
        nc.vector.wait_ge(ss.h, t9)
        i = nc.vector.tensor_sub(S["rhs32"][:], S["vtm"][:], pO[:, po_ * 512: po_ * 512 + 128])
        porot.done(i)
        i = nc.vector.tensor_scalar(S["Wsb"][:], S["rhs32"][:], bcol, None, ALU.mult)
        t13 = vs.on(i, 1)
        yield
        po_ = porot.acquire(nc.tensor)
        nc.tensor.wait_ge(vs.h, t13)
        i = nc.tensor.matmul(pO[:, po_ * 512: po_ * 512 + 128], Pp[:], S["Wsb"][:], start=True, stop=True)
        t14 = pe.on(i, 1)
        nc.scalar.wait_ge(pe.h, t14)
        i = nc.scalar.activation(S["zbf"][:], pO[:, po_ * 512: po_ * 512 + 128], AF.Copy)
        t15z = ss.on(i, 1)
        nc.vector.wait_ge(ss.h, t15z)
        i = nc.vector.tensor_copy(S["Wacc"][:], pO[:, po_ * 512: po_ * 512 + 128])
        tWa = vs.on(i, 1)
        porot.done_at(vs, tWa)
        for ksub in range(3):
            pa = parot.acquire(nc.tensor)
            nc.tensor.wait_ge(ss.h, t15z)
            i = nc.tensor.matmul(pA[:, pa * 512: pa * 512 + 128], S["lnegT"][:], S["zbf"][:],
                                 start=True, stop=True)
            ty = pe.on(i, 1)
            nc.scalar.wait_ge(pe.h, ty)
            i = nc.scalar.activation(S["ybf"][:], pA[:, pa * 512: pa * 512 + 128], AF.Copy)
            t15y = ss.on(i, 1)
            parot.done_at(ss, t15y)
            po_ = porot.acquire(nc.tensor)
            nc.tensor.wait_ge(ss.h, t15y)
            i = nc.tensor.matmul(pO[:, po_ * 512: po_ * 512 + 128], Pp[:], S["ybf"][:],
                                 start=True, stop=True)
            tz = pe.on(i, 1)
            nc.scalar.wait_ge(pe.h, tz)
            i = nc.scalar.activation(S["zbf"][:], pO[:, po_ * 512: po_ * 512 + 128], AF.Copy)
            t15z = ss.on(i, 1)
            nc.vector.wait_ge(ss.h, t15z)
            i = nc.vector.tensor_add(S["Wacc"][:], S["Wacc"][:], pO[:, po_ * 512: po_ * 512 + 128])
            tWa = vs.on(i, 1)
            porot.done_at(vs, tWa)
            yield
        i = nc.vector.tensor_copy(S["Wsb"][:], S["Wacc"][:])
        t15 = vs.on(i, 1)
        pa = parot.acquire(nc.tensor)
        i = nc.tensor.matmul(pA[:, pa * 512: pa * 512 + 128], khs, qhs, start=True, stop=True)
        t16 = pe.on(i, 1)
        nc.vector.wait_ge(pe.h, t16)
        i = nc.vector.tensor_mul(S["Apr"][:], pA[:, pa * 512: pa * 512 + 128], maskui[:])
        t17 = vs.on(i, 1)
        parot.done_at(vs, t17)
        po_ = porot.acquire(nc.tensor)
        nc.tensor.wait_ge(vs.h, t15)
        nc.tensor.wait_ge(vs.h, t17)
        nc.tensor.matmul(pO[:, po_ * 512: po_ * 512 + 128], qhs, Sbsl, start=True, stop=False)
        i = nc.tensor.matmul(pO[:, po_ * 512: po_ * 512 + 128], S["Apr"][:], S["Wsb"][:],
                             start=False, stop=True)
        t18 = pe.on(i, 1)
        pa = parot.acquire(nc.tensor)
        nc.tensor.wait_ge(ss.h, t11)
        i = nc.tensor.matmul(pA[:, pa * 512: pa * 512 + 128], S["ktm"][:], S["Wsb"][:],
                             start=True, stop=True)
        t19 = pe.on(i, 1)
        nc.vector.wait_ge(pe.h, t19)
        i = nc.vector.tensor_add(Ssl, Ssl, pA[:, pa * 512: pa * 512 + 128])
        parot.done(i)
        t20 = vs.on(nc.vector.tensor_copy(Sbsl, Ssl), 1)
        nc.scalar.wait_ge(pe.h, t18)
        i = nc.scalar.activation(S["sq32"][:], pO[:, po_ * 512: po_ * 512 + 128], AF.Square)
        t21 = ss.on(i, 1)
        nc.vector.wait_ge(ss.h, t21)
        t22 = vs.on(nc.vector.reduce_sum(S["rstd"][:, 0:1], S["sq32"][:], AX.X), 1)
        nc.scalar.wait_ge(vs.h, t22)
        nc.scalar.activation(S["rstd"][:, 1:2], S["rstd"][:, 0:1], AF.Ln, bias=epsc[:], scale=i128[:])
        t23 = ss.on(nc.scalar.activation(S["rstd"][:, 2:3], S["rstd"][:, 1:2], AF.Exp, scale=mhc[:]), 1)
        nc.vector.wait_ge(ss.h, t23)
        i = nc.vector.tensor_scalar(S["onbf"][:], pO[:, po_ * 512: po_ * 512 + 128],
                                    S["rstd"][:, 2:3], None, ALU.mult)
        t24 = vs.on(i, 1)
        porot.done_at(vs, t24)
        pts = ptrot[1].acquire(nc.tensor)
        nc.tensor.wait_ge(vs.h, t24)
        i = nc.tensor.transpose(pT1[:], S["onbf"][:], identb[:])
        t25 = pe.on(i, 1)
        nc.vector.wait_ge(pe.h, t25)
        i = nc.vector.tensor_mul(odst[:, ob + h * 128: ob + (h + 1) * 128], pT1[:], ghs)
        t26 = vs.on(i, 1)
        ptrot[1].done_at(vs, t26)
        S["last"] = t26

    for c in range(NTC):
        cs = crot.acquire(nc.sync)
        cb = cs * NH * 128
        clds.inc(cs, nc.sync.dma_start(qc_sb[:, cb:cb + NH * 128].rearrange("p (h t) -> p h t", h=NH),
                                       fm_view(qn_d, 0, NCH, c * 128, 128)))
        clds.inc(cs, nc.sync.dma_start(kc_sb[:, cb:cb + NH * 128].rearrange("p (h t) -> p h t", h=NH),
                                       fm_view(kn_d, 0, NCH, c * 128, 128)))
        clds.inc(cs, nc.sync.dma_start(vc_sb[:, cb:cb + NH * 128].rearrange("p (h t) -> p h t", h=NH),
                                       fm_view(v_d, 0, NCH, c * 128, 128)))
        clds.inc(cs, nc.sync.dma_start(gc_sb[:, cb:cb + NH * 128].rearrange("p (h t) -> p h t", h=NH),
                                       fm_view(gate_d, 0, NCH, c * 128, 128)))
        clds.inc(cs, nc.sync.dma_start(bt_sb[:, cs * 16:(cs + 1) * 16],
                                       beta_d.ap()[c * 128:(c + 1) * 128, :]))
        os_ = orot.acquire(nc.vector)
        ob = os_ * NH * 128
        last_t = None
        for hg in range(0, NH, NS):
            gens = [hsteps(c, cs, os_, hg + j, scr[j]) for j in range(NS)]
            alive = list(gens)
            while alive:
                for ggen in list(alive):
                    try:
                        next(ggen)
                    except StopIteration:
                        alive.remove(ggen)
            for j in range(NS):
                last_t = max(last_t or 0, scr[j]["last"])
        nc.sync.wait_ge(vs.h, last_t)
        odss.inc(os_, nc.sync.dma_start(fm_view(od_d, 0, NCH, c * 128, 128),
                                        odst[:, ob:ob + NH * 128].rearrange("p (h t) -> p h t", h=NH)))
        orot.done_at(*odss.pair(os_))
        if c == NTC - 1:
            pass
        crot.done_at(*odss.pair(os_))
    for sl_ in range(2):
        odss.wait(nc.sync, sl_)



def emit_l1ffn(env, ctx, xn_d, gw_d, uw_d, dw_d, gs_d, us_d, ds_d, x2a_d, xout_d):
    nc = env.nc
    II = 8192
    NIC = II // 128
    sb = lambda n, s, d=F32: ctx.enter_context(nc.sbuf_tensor(n, s, d))
    ld, wld, dld, stm, vs, ss, pe = (env.sem("f1_" + x) for x in
                                     ["ld", "wld", "dld", "st", "v", "s", "pe"])
    HT = 1024
    hstash = sb("f1_h", [128, NIC * HT], BF16)
    gsc = sb("f1_gs", [128, 64])
    usc = sb("f1_us", [128, 64])
    dsc = sb("f1_ds", [128, 16])
    onesb = sb("f1_onesb", [128, 1], BF16)
    onesrf = sb("f1_onesrf", [1, 128])

    nc.gpsimd.memset(onesb[:], 1.0)
    nc.gpsimd.memset(onesrf[:], 1.0)
    ld.on(nc.sync.dma_start(gsc[:], gs_d.ap()), 16)
    ld.on(nc.sync.dma_start(usc[:], us_d.ap()), 16)
    PL = ld.on(nc.sync.dma_start(dsc[:], ds_d.ap()), 16)
    wlds = SlotSem(env, "f1_wlds", 2)
    dlds = SlotSem(env, "f1_dlds", 2)
    xlds = SlotSem(env, "f1_xlds", 2)
    xres_l = SlotSem(env, "f1_xres", 2)
    xsts = SlotSem(env, "f1_xsts", 2)

    wrot = Rot(env, "f1_wrot", 2)
    drot = Rot(env, "f1_drot", 2)
    pgrot = Rot(env, "f1_pgrot", 2)
    pdrot = Rot(env, "f1_pdrot", 2)

    for half in range(2):
        t0 = half * HT
        with ExitStack() as c2:
            xn_sb = c2.enter_context(nc.sbuf_tensor("f1_xn%d" % half, [128, NCH * HT], BF16))
            gwb = c2.enter_context(nc.sbuf_tensor("f1_gw%d" % half, [128, 2 * NCH * 256], BF16))
            gsil = c2.enter_context(nc.sbuf_tensor("f1_gsil%d" % half, [128, 2 * 512], F32))
            ucs = c2.enter_context(nc.sbuf_tensor("f1_uc%d" % half, [128, 2 * 512], F32))
            pg = c2.enter_context(nc.psum_tensor("f1_pg%d" % half, [128, 2 * 512], F32))
            pu = c2.enter_context(nc.psum_tensor("f1_pu%d" % half, [128, 2 * 512], F32))
            if half == 1:
                xsts.wait(nc.sync, 0)
                xsts.wait(nc.gpsimd, 0)
                nc.sync.wait_ge(pe.h, pe.v)
            xlds.inc(half, nc.sync.dma_start(xn_sb[:].rearrange("p (c t) -> p c t", c=NCH),
                                             fm_view(xn_d, 0, NCH, t0, HT)))
            csx = c2.enter_context(nc.sbuf_tensor("f1_csx%d" % half, [1, 2 * 512], F32))
            pxs = c2.enter_context(nc.psum_tensor("f1_pxs%d" % half, [2, 2 * 512], F32))
            xlds.wait(nc.tensor, half)
            for s2 in range(2):
                for kc in range(NCH):
                    i = nc.tensor.matmul(pxs[0:1, s2 * 512:(s2 + 1) * 512], onesb[:],
                                         xn_sb[:, kc * HT + s2 * 512: kc * HT + (s2 + 1) * 512],
                                         start=(kc == 0), stop=(kc == NCH - 1), skip_group_check=True)
            ptv = pe.on(i, 1)
            nc.vector.wait_ge(pe.h, ptv)
            xsv = vs.on(nc.vector.tensor_scalar(csx[:], pxs[0:1, :], -128.0, None, ALU.mult), 1)
            for ic in range(NIC):
                ws = wrot.acquire(nc.gpsimd)
                base = ws * NCH * 256
                wlds.inc(ws, nc.gpsimd.dma_start(
                    gwb[:, base:base + NCH * 256].rearrange("p (c f) -> p c f", c=NCH)[:, :, 0:128],
                    gw_d.ap().rearrange("(c p) f -> p c f", p=128)[:, :, ic * 128:(ic + 1) * 128]))
                wlds.inc(ws, nc.gpsimd.dma_start(
                    gwb[:, base:base + NCH * 256].rearrange("p (c f) -> p c f", c=NCH)[:, :, 128:256],
                    uw_d.ap().rearrange("(c p) f -> p c f", p=128)[:, :, ic * 128:(ic + 1) * 128]))
                for s2 in range(2):
                    pgs = pgrot.acquire(nc.tensor)
                    if ic == 0 and s2 == 0:
                        nc.tensor.wait_ge(vs.h, xsv)
                    wlds.wait(nc.tensor, ws)
                    for kc in range(NCH):
                        nc.tensor.matmul(pg[:, pgs * 512:(pgs + 1) * 512],
                                         gwb[:, base + kc * 256: base + kc * 256 + 128],
                                         xn_sb[:, kc * HT + s2 * 512: kc * HT + (s2 + 1) * 512],
                                         start=(kc == 0), stop=False)
                    nc.tensor.matmul(pg[:, pgs * 512:(pgs + 1) * 512], onesrf[:],
                                     csx[0:1, s2 * 512:(s2 + 1) * 512],
                                     start=False, stop=True, skip_group_check=True)
                    for kc in range(NCH):
                        nc.tensor.matmul(pu[:, pgs * 512:(pgs + 1) * 512],
                                         gwb[:, base + kc * 256 + 128: base + kc * 256 + 256],
                                         xn_sb[:, kc * HT + s2 * 512: kc * HT + (s2 + 1) * 512],
                                         start=(kc == 0), stop=False)
                    i = nc.tensor.matmul(pu[:, pgs * 512:(pgs + 1) * 512], onesrf[:],
                                         csx[0:1, s2 * 512:(s2 + 1) * 512],
                                         start=False, stop=True, skip_group_check=True)
                    ptv = pe.on(i, 1)
                    if s2 == 1:
                        wrot.done_at(pe, ptv)
                    nc.scalar.wait_ge(pe.h, ptv)
                    if ic == 0 and s2 == 0 and half == 0:
                        nc.scalar.wait_ge(ld.h, PL)
                    stq = ss.on(nc.scalar.activation(gsil[:, pgs * 512:(pgs + 1) * 512],
                                                     pg[:, pgs * 512:(pgs + 1) * 512], AF.Silu,
                                                     scale=gsc[:, ic // 4 * 4 + ic % 4: ic // 4 * 4 + ic % 4 + 1]
                                                     if False else gsc[:, ic:ic + 1]), 1)
                    nc.vector.wait_ge(ss.h, stq)
                    nc.vector.tensor_scalar(ucs[:, pgs * 512:(pgs + 1) * 512],
                                            pu[:, pgs * 512:(pgs + 1) * 512],
                                            usc[:, ic:ic + 1], None, ALU.mult)
                    i = nc.vector.tensor_mul(hstash[:, ic * HT + s2 * 512: ic * HT + (s2 + 1) * 512],
                                             gsil[:, pgs * 512:(pgs + 1) * 512],
                                             ucs[:, pgs * 512:(pgs + 1) * 512])
                    hv = vs.on(i, 1)
                    pgrot.done_at(vs, hv)
        with ExitStack() as c2:
            xacc = c2.enter_context(nc.sbuf_tensor("f1_xa%d" % half, [128, 2 * HT], F32))
            dwb = c2.enter_context(nc.sbuf_tensor("f1_dw%d" % half, [128, 2 * 2048], BF16))
            csrow = c2.enter_context(nc.sbuf_tensor("f1_cs%d" % half, [1, 8 * 512], F32))
            pcs = c2.enter_context(nc.psum_tensor("f1_pcs%d" % half, [2, 2 * 512], F32))
            pd = c2.enter_context(nc.psum_tensor("f1_pd%d" % half, [128, 2 * 512], F32))
            nc.sync.wait_ge(pe.h, pe.v)
            nc.sync.wait_ge(vs.h, vs.v)
            nc.gpsimd.wait_ge(pe.h, pe.v)
            nc.gpsimd.wait_ge(vs.h, vs.v)
            csv = vs.v
            for sblk in range(4):
                if sblk > 0:
                    nc.tensor.wait_ge(vs.h, csv)
                for s2 in range(2):
                    for icb in range(16):
                        ic = sblk * 16 + icb
                        i = nc.tensor.matmul(pcs[0:1, s2 * 512:(s2 + 1) * 512], onesb[:],
                                             hstash[:, ic * HT + s2 * 512: ic * HT + (s2 + 1) * 512],
                                             start=(icb == 0), stop=(icb == 15), skip_group_check=True)
                ptv = pe.on(i, 1)
                nc.vector.wait_ge(pe.h, ptv)
                i = nc.vector.tensor_scalar(csrow[:, sblk * 1024:(sblk + 1) * 1024],
                                            pcs[0:1, :], -128.0, None, ALU.mult)
                csv = vs.on(i, 1)
            nc.tensor.wait_ge(vs.h, csv)
            for fc in range(NCH):
                xs = fc % 2
                if fc >= 2 or half == 1:
                    xsts.wait(nc.sync, xs)
                xres_l.inc(xs, nc.sync.dma_start(xacc[:, xs * HT:(xs + 1) * HT],
                                                 fm_view(x2a_d, fc, 1, t0, HT)[:, 0, :]))
                pds = [pdrot.acquire(nc.tensor) for _ in range(2)]
                for sblk in range(4):
                    dsl = drot.acquire(nc.gpsimd)
                    dlds.inc(dsl, nc.gpsimd.dma_start(
                        dwb[:, dsl * 2048:(dsl + 1) * 2048].rearrange("p (c f) -> p c f", c=16),
                        dw_d.ap().rearrange("(c p) f -> p c f", p=128)[:, sblk * 16:(sblk + 1) * 16,
                                                                      fc * 128:(fc + 1) * 128]))
                    dlds.wait(nc.tensor, dsl)
                    for s2 in range(2):
                        for icb in range(16):
                            ic = sblk * 16 + icb
                            iw = nc.tensor.matmul(pd[:, pds[s2] * 512:(pds[s2] + 1) * 512],
                                                  dwb[:, dsl * 2048 + icb * 128: dsl * 2048 + (icb + 1) * 128],
                                                  hstash[:, ic * HT + s2 * 512: ic * HT + (s2 + 1) * 512],
                                                  start=(sblk == 0 and icb == 0), stop=False,
                                                  skip_group_check=True)
                        r0 = (sblk * 2 + s2) * 512
                        i = nc.tensor.matmul(pd[:, pds[s2] * 512:(pds[s2] + 1) * 512], onesrf[:],
                                             csrow[0:1, r0:r0 + 512],
                                             start=False, stop=(sblk == 3), skip_group_check=True)
                    drot.done(iw)
                ptv = pe.on(i, 1)
                for s2 in range(2):
                    nc.vector.wait_ge(pe.h, ptv)
                    xres_l.wait(nc.vector, xs)
                    i = nc.vector.scalar_tensor_tensor(xacc[:, xs * HT + s2 * 512: xs * HT + (s2 + 1) * 512],
                                                       pd[:, pds[s2] * 512:(pds[s2] + 1) * 512],
                                                       dsc[:, fc:fc + 1],
                                                       xacc[:, xs * HT + s2 * 512: xs * HT + (s2 + 1) * 512],
                                                       ALU.mult, ALU.add)
                    av = vs.on(i, 1)
                    pdrot.done_at(vs, av)
                nc.sync.wait_ge(vs.h, av)
                xsts.inc(xs, nc.sync.dma_start(fm_view(xout_d, fc, 1, t0, HT)[:, 0, :],
                                               xacc[:, xs * HT:(xs + 1) * HT]))
            for sl_ in range(2):
                xsts.wait(nc.sync, sl_)
                xsts.wait(nc.gpsimd, sl_)



SH = {
    "x0": ([D, T], F32), "xn0": ([D, T], BF16), "q": ([D, T], BF16),
    "k": ([512, T], BF16), "vaug": ([T, 8, 65], BF16), "oall": ([D, T], BF16),
    "x1a": ([D, T], F32), "x1": ([D, T], F32), "xn1": ([D, T], BF16),
    "qn": ([D, T], BF16), "kn": ([D, T], BF16), "v1": ([D, T], BF16),
    "beta": ([T, 16], F32), "gate": ([D, T], BF16), "od": ([D, T], BF16),
    "x2a": ([D, T], F32), "xout": ([D, T], F32),
    "l0_wqkv_t": ([D, 3072], F32), "l0_bqk": ([128, 20], F32), "l0_bv_bc": ([128, 512], F32),
    "l0_wout_t": ([D, D], F32), "l0_bout": ([128, 16], F32),
    "l0_router_t": ([D, 16], F32), "l0_ebias_bc": ([128, 16], F32),
    "selmat": ([16, 2048], F32), "identf": ([128, 128], F32), "identbf": ([128, 128], BF16),
    "mask_sl": ([128, 128], F32), "mask_ui": ([128, 128], F32),
    "mask_bd": ([128, 128], F32), "mask_bdT": ([128, 128], F32), "mask_blnegT": ([128, 128], F32),
    "l0_gup_t": ([17, D, 1536], F32), "l0_dwn_t": ([17, 768, D], F32),
    "l1_wq": ([D, D], F32), "l1_wk": ([D, D], F32), "l1_wv": ([D, D], F32),
    "l1_wo": ([D, D], F32), "l1_wg": ([D, D], F32), "l1_wb": ([D, 16], F32),
    "l1_cwq": ([128, 64], F32), "l1_cwk": ([128, 64], F32), "l1_cwv": ([128, 64], F32),
    "l1_normw": ([128, 1], F32),
    "l1_gw_t": ([D, 8192], I32), "l1_uw_t": ([D, 8192], I32), "l1_dw_t": ([8192, D], I32),
    "l1_gs": ([128, 64], F32), "l1_us": ([128, 64], F32), "l1_ds": ([128, 16], F32),
}

PHASES = [
    ("ln0", emit_ln, ["x0"], ["xn0"]),
    ("l0qkv", emit_l0qkv, ["xn0", "l0_wqkv_t", "l0_bqk", "l0_bv_bc"], ["q", "k", "vaug"]),
    ("l0att", emit_l0att, ["q", "k", "vaug"], ["oall"]),
    ("l0wo", emit_l0wo, ["oall", "x0", "l0_wout_t", "l0_bout"], ["x1a"]),
    ("l0moe", emit_l0moe, ["xn0", "x1a", "l0_router_t", "l0_ebias_bc", "selmat", "identf",
                           "l0_gup_t", "l0_dwn_t"], ["x1"]),
    ("ln1", emit_ln, ["x1"], ["xn1"]),
    ("l1proj", emit_l1proj, ["xn1", "l1_wq", "l1_wk", "l1_wv", "l1_wb",
                             "l1_cwq", "l1_cwk", "l1_cwv"], ["qn", "kn", "v1", "beta"]),
    ("l1gate", emit_l1gate, ["xn1", "l1_wg", "l1_normw"], ["gate"]),
    ("l1delta", emit_l1delta, ["qn", "kn", "v1", "beta", "gate", "mask_sl", "mask_ui",
                               "mask_bd", "mask_bdT", "mask_blnegT", "identbf"], ["od"]),
    ("l1wo", emit_l1wo, ["od", "x1", "l1_wo"], ["x2a"]),
    ("l1ffn", emit_l1ffn, ["xn1", "l1_gw_t", "l1_uw_t", "l1_dw_t", "l1_gs", "l1_us", "l1_ds",
                           "x2a"], ["xout"]),
]


INTERMEDIATES = ["xn0", "q", "k", "vaug", "oall", "x1a", "x1", "xn1",
                 "qn", "kn", "v1", "beta", "gate", "od", "x2a"]


class NCProxy:

    def __init__(self, nc, tag):
        object.__setattr__(self, "_nc", nc)
        object.__setattr__(self, "_tag", tag)

    def sbuf_tensor(self, name, *a, **k):
        return self._nc.sbuf_tensor(f"{name}__{self._tag}", *a, **k)

    def psum_tensor(self, name, *a, **k):
        return self._nc.psum_tensor(f"{name}__{self._tag}", *a, **k)

    def __getattr__(self, a):
        return getattr(self._nc, a)


def build_fused():
    nc = bass.Bass("TRN2", target_bir_lowering=False)
    dram = {}
    innames = set()
    for name, fn, ins, outs in PHASES:
        for n in ins + outs:
            if n in dram:
                continue
            shape, dt = SH[n]
            if n in INTERMEDIATES:
                dram[n] = nc.dram_tensor(n, list(shape), dt)
            elif n == "xout":
                dram[n] = nc.dram_tensor(n, list(shape), dt, kind="ExternalOutput")
            else:
                dram[n] = nc.dram_tensor(n, list(shape), dt, kind="ExternalInput")
                innames.add(n)
    with ExitStack() as ctx:
        env = Env(nc, ctx)
        for k, (name, fn, ins, outs) in enumerate(PHASES):
            env.nc = NCProxy(nc, f"{k}")
            with nc.named_scope(name):
                with ExitStack() as pctx:
                    fn(env, pctx, *[dram[n] for n in ins], *[dram[n] for n in outs])
                nc.all_engine_barrier()
        env.nc = nc
    return nc, sorted(innames)


def build_phase(name):
    for nm, fn, ins, outs in PHASES:
        if nm == name:
            break
    else:
        raise KeyError(name)
    nc = bass.Bass("TRN2", target_bir_lowering=False)
    dins = [nc.dram_tensor(n, list(SH[n][0]), SH[n][1], kind="ExternalInput") for n in ins]
    douts = [nc.dram_tensor(n, list(SH[n][0]), SH[n][1], kind="ExternalOutput") for n in outs]
    with ExitStack() as ctx:
        env = Env(nc, ctx)
        fn(env, ctx, *dins, *douts)
    return nc, ins, outs


def kernel(**inputs):
    st = stage(inputs)
    nc, innames = build_fused()
    in_maps = [{n: np.ascontiguousarray(st[n][c] if n == "x0" else st[n]) for n in innames}
               for c in range(NCORE)]
    res = run_bass_kernel_spmd(nc, in_maps, core_ids=list(range(NCORE)))
    return np.stack([np.asarray(res.results[c]["xout"], np.float32).T for c in range(NCORE)])


def kernel_unfused(**inputs):
    st = stage(inputs)
    percore = {"x0": st["x0"]}

    def fetch(n, c):
        if n in percore:
            return percore[n][c]
        return st[n]

    for name, _, _, _ in PHASES:
        nc, innames, outnames = build_phase(name)
        in_maps = [{n: np.ascontiguousarray(fetch(n, c)) for n in innames} for c in range(NCORE)]
        res = run_bass_kernel_spmd(nc, in_maps, core_ids=list(range(NCORE)))
        for n in outnames:
            percore[n] = [res.results[c][n] for c in range(NCORE)]
    return np.stack([np.asarray(percore["xout"][c], np.float32).T for c in range(NCORE)])
